# version 27
# baseline (speedup 1.0000x reference)
import numpy as np
import ml_dtypes
import concourse.bass as bass
import concourse.mybir as mybir
from concourse.bass_utils import run_bass_kernel_spmd

F32 = mybir.dt.float32
BF16 = mybir.dt.bfloat16
AF = mybir.ActivationFunctionType
ALU = mybir.AluOpType
AX = mybir.AxisListType

D = 1024
N = 2048
HEADS = 16
DH = 64
NT = 16
DT = 8
NE = 8
RK = 16
ER = NE * RK
M0 = 30.0
NEG = -1e30
GROUPS = 4
GH = 4
VSTR = HEADS * 65
CW = 512

ENGINES = ("tensor", "scalar", "vector", "gpsimd", "sync")

_DEBUG = False


def head_is_rope(h):
    return (h % 4) >= 2


def head_is_decay(h):
    return h % 2 == 1


def build_consts():
    c = {}
    pos = np.arange(N, dtype=np.float64)
    pm = pos - (N - 1)
    pm_hi = pm.astype(ml_dtypes.bfloat16).astype(np.float64)
    pm_lo = pm - pm_hi
    posm = np.zeros((65, 2 * N), dtype=np.float64)
    posm[64, 0:N] = pm_hi
    posm[64, N:2 * N] = pm_lo
    c["c_posm"] = posm.astype(ml_dtypes.bfloat16)
    p2 = np.arange(128)[:, None]
    t2 = np.arange(16)[None, :]
    c["c_iotam"] = ((128.0 * t2 + p2) - (N - 1)).astype(np.float32)
    u = np.arange(128)[None, :]
    c["c_masktri"] = np.where(u < np.arange(128)[:, None], NEG, 0.0).astype(
        ml_dtypes.bfloat16)
    inv_freq = 10000.0 ** (-(np.arange(0, DH, 2, dtype=np.float64) / DH))
    theta = pos[None, :] * inv_freq[:, None]
    cosv, sinv = np.cos(theta), np.sin(theta)
    c["c_trigC"] = np.concatenate([cosv, cosv], axis=0).astype(ml_dtypes.bfloat16)
    c["c_trigS"] = np.concatenate([-sinv, sinv], axis=0).astype(ml_dtypes.bfloat16)
    c["c_ident"] = np.eye(128).astype(ml_dtypes.bfloat16)
    return c


def perm_qk_cols():
    idx = np.zeros(D, dtype=np.int64)
    for h in range(HEADS):
        base = h * DH
        idx[base:base + 32] = base + np.arange(0, DH, 2)
        idx[base + 32:base + 64] = base + np.arange(1, DH, 2)
    return idx


def prep_weights(w_qkv, w_o, log_xi, w_ff, gate_w, lora_U, lora_V):
    idx = perm_qk_cols()
    wqkv = np.asarray(w_qkv, dtype=np.float32)
    d = {}
    d["w_qk"] = np.ascontiguousarray(
        np.concatenate([wqkv[:, 0:D][:, idx], wqkv[:, D:2 * D][:, idx]], axis=1))
    d["w_v"] = np.ascontiguousarray(wqkv[:, 2 * D:3 * D])
    d["w_o"] = np.ascontiguousarray(np.asarray(w_o, dtype=np.float32))
    d["w_ffT"] = np.ascontiguousarray(np.asarray(w_ff, dtype=np.float32).T)
    d["w_gT"] = np.ascontiguousarray(np.asarray(gate_w, dtype=np.float32).T)
    d["w_uT"] = np.ascontiguousarray(np.asarray(lora_U, dtype=np.float32).reshape(ER, D).T)
    d["w_vlT"] = np.ascontiguousarray(
        np.asarray(lora_V, dtype=np.float32).transpose(0, 2, 1).reshape(ER, D))
    d["logxi"] = np.ascontiguousarray(np.asarray(log_xi, dtype=np.float32).reshape(1, HEADS))
    return d


class Prog:

    def __init__(self):
        self.ops = []
        self.last_writer = {}
        self.readers = {}

    def op(self, engine, builder, reads=(), writes=(), dma_sem=None, tag=""):
        i = len(self.ops)
        raw = set()
        war = set()
        for r in reads:
            w = self.last_writer.get(r)
            if w is not None:
                raw.add(w)
        for r in writes:
            w = self.last_writer.get(r)
            if w is not None:
                raw.add(w)
            war.update(self.readers.get(r, ()))
        raw.discard(i)
        war.discard(i)
        self.ops.append(dict(i=i, engine=engine, builder=builder, raw=raw, war=war,
                             dma_sem=dma_sem, tag=tag, needed=False))
        for r in reads:
            self.readers.setdefault(r, []).append(i)
        for r in writes:
            self.last_writer[r] = i
            self.readers[r] = []
        return i

    def _need_edge(self, p, rec, is_raw):
        if p["dma_sem"] is not None:
            return True
        if p["engine"] != rec["engine"]:
            return True
        if p["engine"] == "tensor":
            return False
        return True

    def finalize(self):
        ops = self.ops
        for rec in ops:
            for d in rec["raw"]:
                if self._need_edge(ops[d], rec, True):
                    ops[d]["needed"] = True
            for d in rec["war"]:
                if self._need_edge(ops[d], rec, False):
                    ops[d]["needed"] = True
        eng_count = {e: 0 for e in ENGINES}
        dma_count = {}
        for rec in ops:
            if rec["dma_sem"] is not None:
                dma_count[rec["dma_sem"]] = dma_count.get(rec["dma_sem"], 0) + 16
                rec["ev"] = (rec["dma_sem"], dma_count[rec["dma_sem"]])
                rec["inc"] = (rec["dma_sem"], 16)
            elif rec["needed"]:
                eng_count[rec["engine"]] += 1
                rec["ev"] = ("e_" + rec["engine"], eng_count[rec["engine"]])
                rec["inc"] = ("e_" + rec["engine"], 1)
            else:
                rec["ev"] = None
                rec["inc"] = None
        hw = {e: {} for e in ENGINES}
        for rec in ops:
            e = rec["engine"]
            waits = {}
            for d, is_raw in [(d2, True) for d2 in sorted(rec["raw"])] + \
                             [(d2, False) for d2 in sorted(rec["war"])]:
                p = ops[d]
                if p["ev"] is None or not self._need_edge(p, rec, is_raw):
                    continue
                sn, v = p["ev"]
                if waits.get(sn, 0) < v:
                    waits[sn] = v
            wl = []
            for sn, v in sorted(waits.items()):
                if hw[e].get(sn, 0) < v:
                    hw[e][sn] = v
                    wl.append((sn, v))
            rec["waits"] = wl
        return dma_count


def build():
    nc = bass.Bass("TRN2", target_bir_lowering=False)

    x_hbm = nc.dram_tensor("x", [N, D], F32, kind="ExternalInput").ap()
    y_hbm = nc.dram_tensor("y", [N, D], F32, kind="ExternalOutput").ap()
    w_qk = nc.dram_tensor("w_qk", [D, 2 * D], F32, kind="ExternalInput").ap()
    w_v = nc.dram_tensor("w_v", [D, D], F32, kind="ExternalInput").ap()
    w_o = nc.dram_tensor("w_o", [D, D], F32, kind="ExternalInput").ap()
    w_ffT = nc.dram_tensor("w_ffT", [D, D], F32, kind="ExternalInput").ap()
    w_gT = nc.dram_tensor("w_gT", [D, NE], F32, kind="ExternalInput").ap()
    w_uT = nc.dram_tensor("w_uT", [D, ER], F32, kind="ExternalInput").ap()
    w_vlT = nc.dram_tensor("w_vlT", [ER, D], F32, kind="ExternalInput").ap()
    logxi_h = nc.dram_tensor("logxi", [1, HEADS], F32, kind="ExternalInput").ap()
    c_posm = nc.dram_tensor("c_posm", [65, 2 * N], BF16, kind="ExternalInput").ap()
    c_masktri = nc.dram_tensor("c_masktri", [128, 128], BF16, kind="ExternalInput").ap()
    c_iotam = nc.dram_tensor("c_iotam", [128, 16], F32, kind="ExternalInput").ap()
    c_trigC = nc.dram_tensor("c_trigC", [64, N], BF16, kind="ExternalInput").ap()
    c_trigS = nc.dram_tensor("c_trigS", [64, N], BF16, kind="ExternalInput").ap()
    c_ident = nc.dram_tensor("c_ident", [128, 128], BF16, kind="ExternalInput").ap()

    dbg = {}
    if _DEBUG:
        dbg["xnT"] = nc.dram_tensor("dbg_xnT", [128, DT * N], BF16, kind="ExternalOutput").ap()
        dbg["attok"] = nc.dram_tensor("dbg_attok", [128, NT * D], BF16, kind="ExternalOutput").ap()
        dbg["gate"] = nc.dram_tensor("dbg_gate", [128, NT * NE], F32, kind="ExternalOutput").ap()
        dbg["tsb"] = nc.dram_tensor("dbg_tsb", [128, NT * ER], BF16, kind="ExternalOutput").ap()
        dbg["vaug"] = nc.dram_tensor("dbg_vaug", [128, NT * VSTR], BF16, kind="ExternalOutput").ap()
        dbg["qt"] = nc.dram_tensor("dbg_qt", [128, 16384], BF16, kind="ExternalOutput").ap()
        dbg["num"] = nc.dram_tensor("dbg_num", [128, 1536], BF16, kind="ExternalOutput").ap()
        dbg["rden"] = nc.dram_tensor("dbg_rden", [128, 8], F32, kind="ExternalOutput").ap()

    x_t = x_hbm.rearrange("(n p) d -> n p d", p=128)
    y_t = y_hbm.rearrange("(n p) d -> n p d", p=128)

    ctx = []

    def sbuf(name, shape, dtype):
        cm = nc.sbuf_tensor(name, shape, dtype)
        t = cm.__enter__()
        ctx.append(cm)
        return t

    def psum_t(name, shape, dtype):
        cm = nc.psum_tensor(name, shape, dtype)
        t = cm.__enter__()
        ctx.append(cm)
        return t

    def semh(name):
        cm = nc.semaphore(name)
        s = cm.__enter__()
        ctx.append(cm)
        return s

    xnT = sbuf("xnT", [128, DT * N], BF16)
    vaug = sbuf("vaug", [128, NT * VSTR], BF16)
    attok = sbuf("attok", [128, NT * D], BF16)
    aQK = sbuf("aQK", [128, 16384], BF16)
    aW = sbuf("aW", [128, 16384], BF16)
    aS = sbuf("aS", [128, 14336], BF16)
    numb = sbuf("numb", [128, 2048], BF16)
    tsb = sbuf("tsb", [128, NT * ER], BF16)
    gate_all = sbuf("gate_all", [128, NT * NE], F32)
    gate_raw = sbuf("gate_raw", [128, NT * NE], F32)
    wg_bf = sbuf("wg_bf", [128, DT * NE], BF16)
    wu_bf = sbuf("wu_bf", [128, DT * ER], BF16)
    wvl_bf = sbuf("wvl_bf", [128, D], BF16)
    tg_b = sbuf("tg_b", [128, 2 * ER], BF16)
    tgT_b = sbuf("tgT_b", [128, 2 * ER], BF16)
    stats = sbuf("stats", [128, 12], F32)
    mvv = sbuf("mvv", [128, 16], F32)
    gsum = sbuf("gsum", [128, 32], F32)
    rden = sbuf("rden", [128, 8], F32)
    invxi = sbuf("invxi", [1, HEADS], F32)
    invxib = sbuf("invxib", [128, HEADS], F32)
    biasc = sbuf("biasc", [128, 1], F32)
    zeroc = sbuf("zeroc", [128, 1], F32)
    ones1 = sbuf("ones1", [1, 128], F32)
    masktri = sbuf("masktri", [128, 128], BF16)
    iotam = sbuf("iotam", [128, 16], F32)
    biasall = sbuf("biasall", [128, HEADS * 16], F32)
    identb = sbuf("identb", [128, 128], BF16)

    xt_f32 = aQK[:, 0:8192].bitcast(F32)
    xr_f32 = vaug[:, 0:4096].bitcast(F32)
    QT = [aQK[0:68, 2048 * s:2048 * (s + 1)] for s in range(GH)]
    KT = [aQK[0:68, 2048 * (4 + s):2048 * (5 + s)] for s in range(GH)]
    attT = aQK
    wv_bf = aW[:, 0:8192]
    trigC = aW[0:64, 8192:10240]
    trigS = aW[0:64, 10240:12288]
    posm = aW[0:65, 12288:16384]
    wo_bf = aW[:, 0:8192]
    wff_bf = aW[:, 8192:16384]
    stage = aS[:, 0:2048]
    hbsB = aS[0:64, 2048:4096]
    swapA = aS[0:64, 4096:6144]
    swapB = aS[0:64, 6144:8192]
    ropt1 = aS[0:64, 8192:10240]
    wqk_bf = aS[:, 10240:14336]
    gbuf0 = aS[:, 10240:12288].bitcast(F32)
    gbuf1 = aS[:, 12288:14336].bitcast(F32)
    attT0 = aS[:, 0:8192]
    attT1 = aQK[:, 0:8192]

    psA = psum_t("psA", [128, 2048], F32)
    psB = psum_t("psB", [128, 2048], F32)

    def kA(lo, hi):
        return [("pA", b) for b in range(lo // 512, (hi + 511) // 512)]

    def kB(lo, hi):
        return [("pB", b) for b in range(lo // 512, (hi + 511) // 512)]

    sems = {}
    for e in ENGINES:
        sems["e_" + e] = semh("e_" + e)
    for s_ in ("d_scal", "d_sync", "d_wqk", "d_wv0", "d_wv1", "d_wv2", "d_wv3",
               "d_wv4", "d_wv5", "d_wv6", "d_wv7",
               "d_wo", "d_wff", "d_wg",
               "d_wu", "d_wvl", "d_misc", "d_hb", "d_swA", "d_swB", "d_kb",
               "d_x0", "d_x1"):
        sems[s_] = semh(s_)

    P = Prog()

    def dma(engine, dst, src, dsem, reads=(), writes=(), tag="", transpose=False):
        if transpose:
            return P.op(engine, lambda e, d=dst, s=src: e.dma_start(d, s, transpose=True),
                        reads=reads, writes=writes, dma_sem=dsem, tag=tag)
        return P.op(engine, lambda e, d=dst, s=src: e.dma_start(d, s),
                    reads=reads, writes=writes, dma_sem=dsem, tag=tag)

    def vop(builder, reads=(), writes=(), tag=""):
        return P.op("vector", builder, reads, writes, tag=tag)

    def sop(builder, reads=(), writes=(), tag=""):
        return P.op("scalar", builder, reads, writes, tag=tag)

    def top(builder, reads=(), writes=(), tag=""):
        return P.op("tensor", builder, reads, writes, tag=tag)

    CONSTK = [("posm",), ("masktri",), ("ident",), ("iotam",), ("trigC",), ("trigS",), ("logxi",)]

    wqk_r = w_qk.rearrange("(dt p) f -> p dt f", p=128)
    for dt2 in range(DT):
        dma("gpsimd", wg_bf[:, dt2 * NE:(dt2 + 1) * NE],
            w_gT.rearrange("(dt p) f -> dt p f", p=128)[dt2], "d_wg", writes=[("wg",)])
    wv_r = w_v.rearrange("(dt p) f -> p dt f", p=128)
    for dt2 in range(DT):
        dma("gpsimd", wv_bf[:, dt2 * D:(dt2 + 1) * D], wv_r[:, dt2],
            "d_wv%d" % dt2, writes=[("wv", dt2)])
    def emit_wqk_dmas(g):
        qs = 256 * g
        dstv = wqk_bf.rearrange("p (dt f) -> p dt f", dt=DT)
        dma("gpsimd", dstv[:, :, 0:256], wqk_r[:, :, qs:qs + 256], "d_wqk",
            writes=[("wqkbuf",)], tag=f"wqkQ{g}")
        dma("gpsimd", dstv[:, :, 256:512], wqk_r[:, :, D + qs:D + qs + 256], "d_wqk",
            writes=[("wqkbuf",)], tag=f"wqkK{g}")

    def emit_wqk_bar(g):
        P.op("vector", lambda e: e.engine_nop(), reads=[("wqkbuf",)],
             writes=[("wqkgrp", g)], tag=f"wqkbar{g}")

    wu_r = w_uT.rearrange("(dt p) f -> p dt f", p=128)
    dma("gpsimd", wu_bf.rearrange("p (dt f) -> p dt f", dt=DT), wu_r, "d_wu",
        writes=[("wu",)])
    dma("gpsimd", wvl_bf[:], w_vlT, "d_wvl", writes=[("wvl",)])
    dma("gpsimd", posm, c_posm, "d_misc", writes=[("posm",)])
    dma("gpsimd", masktri[:], c_masktri, "d_misc", writes=[("masktri",)])
    dma("gpsimd", iotam[:], c_iotam, "d_misc", writes=[("iotam",)])
    dma("gpsimd", identb[:], c_ident, "d_misc", writes=[("ident",)])
    dma("gpsimd", trigC, c_trigC, "d_misc", writes=[("trigC",)])
    dma("gpsimd", trigS, c_trigS, "d_misc", writes=[("trigS",)])
    dma("gpsimd", invxi[:], logxi_h, "d_misc", writes=[("logxi",)])
    P.op("gpsimd", lambda e: e.engine_nop(), reads=list(CONSTK), writes=list(CONSTK),
         tag="constbar")
    emit_wqk_dmas(0)

    def emit_wo_load():
        dma("gpsimd", wo_bf.rearrange("p (dt f) -> p dt f", dt=DT),
            w_o.rearrange("(dt p) f -> p dt f", p=128), "d_wo",
            writes=[("wo",)] + [("wv", dt) for dt in range(DT)], tag="wo")

    def emit_wff_load():
        dma("gpsimd", wff_bf.rearrange("p (dt f) -> p dt f", dt=DT),
            w_ffT.rearrange("(dt p) f -> p dt f", p=128), "d_wff",
            writes=[("wff",), ("trigC",), ("trigS",), ("posm",)], tag="wff")

    vop(lambda e: e.memset(biasc[:], -M0), writes=[("biasc",)])
    vop(lambda e: e.memset(zeroc[:], 0.0), writes=[("zeroc",)])
    vop(lambda e: e.memset(ones1[:], 1.0), writes=[("ones1",)])
    def emit_qkrow_init():
        for s_ in range(GH):
            P.op("gpsimd", lambda e, s2=s_: e.memset(QT[s2][64:65, :], 8.0),
                 reads=[("xt", 0), ("xt", 1), ("xt", 2), ("xt", 3)],
                 writes=[("QTrow", s_)])
            P.op("gpsimd", lambda e, s2=s_: e.memset(KT[s2][64:65, :], 1.0),
                 reads=[("xt", 0), ("xt", 1), ("xt", 2), ("xt", 3)],
                 writes=[("KTrow", s_)])
    sop(lambda e: e.activation(invxi[:], invxi[:], AF.Exp, bias=zeroc[0:1, 0:1], scale=-1.0),
        reads=[("logxi",), ("zeroc",)], writes=[("invxi",)], tag="invxi")
    top(lambda e: e.matmul(psB[:, 0:16], ones1[:], invxi[:], start=True, stop=True),
        reads=[("ones1",), ("invxi",)], writes=kB(0, 16), tag="bcast")
    vop(lambda e: e.tensor_copy(invxib[:], psB[:, 0:16]),
        reads=kB(0, 16), writes=[("invxib",)])
    for h in range(HEADS):
        if head_is_decay(h):
            vop(lambda e, h2=h: e.tensor_scalar(
                biasall[:, 16 * h2:16 * h2 + 16], iotam[:],
                invxib[:, h2:h2 + 1], -M0, ALU.mult, ALU.add),
                reads=[("iotam",), ("invxib",)], writes=[("biasall", h)], tag=f"bias{h}")
    def emit_A(i):
        r = i % 4
        xv = xt_f32[:, r * D:r * D + D]
        dma("scalar", xv, x_t[i], "d_x0" if i % 2 == 0 else "d_x1",
            reads=[("xt", r)], writes=[("xld", i)], tag=f"xld{i}")
        mv = mvv[:, r * 4:r * 4 + 4]
        vop(lambda e, b=xv: e.bn_stats(stats[:, 0:6], b[:, 0:512]),
            reads=[("xld", i)], writes=[("st0",)], tag=f"bns{i}a")
        vop(lambda e, b=xv: e.bn_stats(stats[:, 6:12], b[:, 512:1024]),
            reads=[("xld", i)], writes=[("st1",)], tag=f"bns{i}b")
        vop(lambda e, m=mv: e.bn_aggr(m[:, 0:2], stats[:, 0:12]),
            reads=[("st0",), ("st1",)], writes=[("mv", r)], tag=f"bna{i}")
        vop(lambda e, m=mv: e.tensor_scalar_add(m[:, 2:3], m[:, 1:2], 1e-5),
            reads=[("mv", r)], writes=[("mv2", r)])
        sop(lambda e, m=mv: e.activation(m[:, 3:4], m[:, 2:3], AF.Ln,
                                         bias=zeroc[:, 0:1], scale=1.0),
            reads=[("mv2", r), ("zeroc",)], writes=[("mv3", r)], tag=f"ln{i}")
        sop(lambda e, m=mv: e.activation(m[:, 2:3], m[:, 3:4], AF.Exp,
                                         bias=zeroc[:, 0:1], scale=-0.5),
            reads=[("mv3", r), ("zeroc",)], writes=[("rstd", r)], tag=f"exr{i}")
        vop(lambda e, m=mv: e.tensor_scalar(
            m[:, 1:2], m[:, 0:1], m[:, 2:3], -1.0, ALU.mult, ALU.mult),
            reads=[("mv", r), ("rstd", r)], writes=[("mvb", r)], tag=f"nm{i}")

    def emit_B(i):
        r = i % 4
        xv = xt_f32[:, r * D:r * D + D]
        mv = mvv[:, r * 4:r * 4 + 4]
        xnv = attok[:, i * D:(i + 1) * D]
        sop(lambda e, m=mv, a=xnv, b=xv: e.activation(
            a, b, AF.Identity, bias=m[:, 1:2], scale=m[:, 2:3]),
            reads=[("xld", i), ("mvb", r), ("rstd", r)],
            writes=[("xnk", i), ("xt", r)], tag=f"xn{i}")
        dstT = xnT[:].rearrange("p (d n) -> p d n", n=N)[:, :, 128 * i:128 * i + 128]
        dma("sync", dstT, xnv, "d_sync", reads=[("xnk", i), ("xbar",)],
            writes=[("xnT", i), ("xbar",)], tag=f"xT{i}", transpose=True)

    def emit_C(i):
        r = i % 2
        vlo = r * 1024
        glo = 512 * (1 + r)
        for dt2 in range(DT):
            lhs = xnT[:, dt2 * N + 128 * i: dt2 * N + 128 * i + 128]
            for c in range(2):
                top(lambda e, l=lhs, d2=dt2, c2=c, v=vlo: e.matmul(
                    psA[:, v + 512 * c2:v + 512 * c2 + 512], l,
                    wv_bf[:, d2 * D + 512 * c2:d2 * D + 512 * c2 + 512],
                    start=(d2 == 0), stop=(d2 == DT - 1)),
                    reads=[("xnT", i), ("wv", dt2)], writes=[("pA", vlo // 512 + c)],
                    tag=f"v{i}_{dt2}_{c}")
        for dt2 in range(DT):
            lhs = xnT[:, dt2 * N + 128 * i: dt2 * N + 128 * i + 128]
            top(lambda e, l=lhs, d2=dt2, g2=glo: e.matmul(
                psB[:, g2:g2 + NE], l, wg_bf[:, d2 * NE:d2 * NE + NE],
                start=(d2 == 0), stop=(d2 == DT - 1)),
                reads=[("xnT", i), ("wg",)], writes=[("pB", glo // 512)], tag=f"g{i}_{dt2}")
        for dt2 in range(DT):
            lhs = xnT[:, dt2 * N + 128 * i: dt2 * N + 128 * i + 128]
            top(lambda e, l=lhs, d2=dt2, g2=glo: e.matmul(
                psB[:, g2 + NE:g2 + NE + ER], l, wu_bf[:, d2 * ER:d2 * ER + ER],
                start=(d2 == 0), stop=(d2 == DT - 1)),
                reads=[("xnT", i), ("wu",)], writes=[("pB", glo // 512)], tag=f"t{i}_{dt2}")

    def emit_E(i):
        r = i % 2
        vlo = r * 1024
        glo = 512 * (1 + r)
        vsrc = psA[:, vlo:vlo + 1024].rearrange("p (h v) -> p h v", v=64)
        vdst = vaug[:, i * VSTR:(i + 1) * VSTR].rearrange("p (h u) -> p h u", u=65)[:, :, 0:64]
        vop(lambda e, d=vdst, s=vsrc: e.tensor_copy(d, s),
            reads=[("pA", vlo // 512), ("pA", vlo // 512 + 1)],
            writes=[("vaug", i)], tag=f"vev{i}")
        vop(lambda e, i2=i, g2=glo: e.tensor_copy(tsb[:, i2 * ER:(i2 + 1) * ER],
                                                  psB[:, g2 + NE:g2 + NE + ER]),
            reads=[("pB", glo // 512)], writes=[("tsb", i)], tag=f"tev{i}")
        vop(lambda e, i2=i, g2=glo: e.tensor_copy(gate_raw[:, i2 * NE:(i2 + 1) * NE],
                                                  psB[:, g2:g2 + NE]),
            reads=[("pB", glo // 512)], writes=[("graw", i)], tag=f"gev{i}")

    for i in range(NT + 3):
        if i < NT:
            emit_A(i)
        if 1 <= i <= NT:
            emit_B(i - 1)
        if 3 <= i <= NT + 2:
            emit_C(i - 3)
            emit_E(i - 3)

    for t in range(NT):
        ap = vaug[:, t * VSTR:(t + 1) * VSTR].rearrange("p (h u) -> p h u", u=65)[:, :, 64:65]
        vop(lambda e, a=ap: e.memset(a, 1.0), writes=[("vone", t)], tag=f"vone{t}")

    sop(lambda e: e.activation(gate_all[:], gate_raw[:], AF.Exp, bias=zeroc[:, 0:1],
                               scale=1.0),
        reads=[("graw", i) for i in range(NT)] + [("zeroc",)], writes=[("gexp",)],
        tag="gexp")
    vop(lambda e: e.reduce_sum(gsum[:, 0:16],
                               gate_all[:].rearrange("p (t e) -> p t e", e=NE), AX.X),
        reads=[("gexp",)], writes=[("gsum",)], tag="gsum")
    vop(lambda e: e.reciprocal(gsum[:, 16:32], gsum[:, 0:16]),
        reads=[("gsum",)], writes=[("grec",)], tag="grec")
    vop(lambda e: e.tensor_tensor(
        gate_all[:].rearrange("p (t e) -> p t e", e=NE),
        gate_all[:].rearrange("p (t e) -> p t e", e=NE),
        gsum[:, 16:32].rearrange("p (t u) -> p t u", u=1).broadcast_to([128, NT, NE]),
        ALU.mult),
        reads=[("gexp",), ("grec",)], writes=[("gate", i) for i in range(NT)], tag="gfin")

    emit_wo_load()

    rot_counter = [0]
    emit_wqk_bar(0)
    emit_qkrow_init()
    for g in range(GROUPS):
        if g > 0:
            emit_wqk_dmas(g)
            emit_wqk_bar(g)
        for fi, ft in enumerate((0, 2, 1, 3)):
            isq = ft < 2
            ps, pk = (psA, [("pA", b) for b in range(4)]) if fi % 2 == 0 else \
                     (psB, [("pB", b) for b in range(4)])
            hA = 4 * g + (ft % 2) * 2
            hB = hA + 1
            slotA, slotB = hA - 4 * g, hB - 4 * g
            wcol = (0 if isq else 256) + (ft % 2) * 128
            for dt2 in range(DT):
                lhs = wqk_bf[:, dt2 * 512 + wcol: dt2 * 512 + wcol + 128]
                for c in range(4):
                    top(lambda e, p=ps, l=lhs, d2=dt2, c2=c: e.matmul(
                        p[:, 512 * c2:512 * c2 + 512], l,
                        xnT[:, d2 * N + 512 * c2: d2 * N + 512 * c2 + 512],
                        start=(d2 == 0), stop=(d2 == DT - 1)),
                        reads=[("wqkgrp", g), ("wqkbuf",)] + [("xnT", tt) for tt in range(NT)],
                        writes=pk, tag=f"qk{g}_{ft}_{dt2}_{c}")
            sop(lambda e, p=ps: e.copy(stage[:, 0:1024], p[:, 0:1024]),
                reads=pk, writes=[("stage",)], tag=f"stgA{g}_{ft}")
            vop(lambda e, p=ps: e.tensor_copy(stage[:, 1024:2048], p[:, 1024:2048]),
                reads=pk, writes=[("stage",)], tag=f"stgB{g}_{ft}")
            bufA = QT[slotA] if isq else KT[slotA]
            bufB = QT[slotB] if isq else KT[slotB]
            bkA = ("QTm" if isq else "KTm", slotA)
            bkB = ("QTm" if isq else "KTm", slotB)
            if head_is_rope(hA):
                dma("scalar", hbsB[0:64, :], stage[64:128, :], "d_hb",
                    reads=[("stage",)], writes=[("hbsB",)], tag=f"hb{g}_{ft}")
                dma("scalar", swapA[0:32, :], stage[32:64, :], "d_swA",
                    reads=[("stage",)], writes=[("swA",)], tag=f"sw1{g}_{ft}")
                dma("scalar", swapA[32:64, :], stage[0:32, :], "d_swA",
                    reads=[("stage",)], writes=[("swA",)], tag=f"sw2{g}_{ft}")
                dma("scalar", swapB[0:32, :], stage[96:128, :], "d_swB",
                    reads=[("stage",)], writes=[("swB",)], tag=f"sw3{g}_{ft}")
                dma("scalar", swapB[32:64, :], stage[64:96, :], "d_swB",
                    reads=[("stage",)], writes=[("swB",)], tag=f"sw4{g}_{ft}")
                for (base, swap, buf, bk, bask, swk) in (
                        (stage, swapA, bufA, bkA, ("stage",), ("swA",)),
                        (hbsB, swapB, bufB, bkB, ("hbsB",), ("swB",))):
                    vop(lambda e, b=buf, s=base: e.tensor_mul(
                        b[0:64, :], s[0:64, :], trigC[0:64, :]),
                        reads=[bask, ("trigC",)], writes=[bk], tag="rp_a")
                    vop(lambda e, s=swap, sk2=swk: e.tensor_mul(
                        ropt1[0:64, :], s[0:64, :], trigS[0:64, :]),
                        reads=[swk, ("trigS",)], writes=[("rt1",)], tag="rp_b")
                    vop(lambda e, b=buf: e.tensor_add(
                        b[0:64, :], ropt1[0:64, :], b[0:64, :]),
                        reads=[("rt1",), bk], writes=[bk], tag="rp_c")
            else:
                dma("scalar", bufB[0:64, :], stage[64:128, :], "d_hb",
                    reads=[("stage",)], writes=[bkB], tag=f"cpB{g}_{ft}")
                vop(lambda e, b=bufA: e.tensor_copy(b[0:64, :], stage[0:64, :]),
                    reads=[("stage",)], writes=[bkA], tag=f"cpA{g}_{ft}")
        for s_ in range(GH):
            h = 4 * g + s_
            if head_is_decay(h):
                vop(lambda e, s2=s_, h2=h: e.tensor_scalar(
                    QT[s2][64:65, :], posm[64:65, 0:N], invxib[64:65, h2:h2 + 1],
                    -8.0, ALU.mult, ALU.mult),
                    reads=[("posm",), ("invxib",), ("QTm", s_)],
                    writes=[("QTrow", s_)], tag=f"qrow{g}_{s_}")


        if _DEBUG and g == 0:
            dma("gpsimd", dbg["qt"], aQK[:], "d_misc",
                reads=[("QTm", s) for s in range(GH)] + [("KTm", s) for s in range(GH)]
                + [("QTrow", s) for s in range(GH)] + [("KTrow", s) for s in range(GH)],
                tag="dbgQ")

        for s_ in range(GH):
            for cch in range(4):
                h = 4 * g + s_
                decay = head_is_decay(h)
                qk_reads = [("QTm", s_), ("KTm", s_), ("QTrow", s_), ("KTrow", s_)]
                pend = []
                cs = 512 * cch
                for t in range(4 * cch + 4):
                    rot = rot_counter[0] % 3
                    rot_counter[0] += 1
                    plo = 512 * rot
                    sk = [("pB", rot)]
                    pre = max(0, 128 * t - cs)
                    top(lambda e, p=plo, t2=t, s2=s_, c2=cs: e.matmul(
                        psB[:, p:p + CW], KT[s2][0:65, 128 * t2:128 * t2 + 128],
                        QT[s2][0:65, c2:c2 + CW], start=True, stop=True),
                        reads=qk_reads, writes=sk, tag=f"sc{h}_{cch}_{t}")
                    if t >= 4 * cch:
                        vop(lambda e, p=plo, o=pre: e.tensor_add(
                            psB[:, p + o:p + o + 128], psB[:, p + o:p + o + 128],
                            masktri[:]),
                            reads=sk + [("masktri",)], writes=sk, tag=f"mk{h}_{cch}_{t}")
                    if decay:
                        bap = biasall[:, 16 * h + t:16 * h + t + 1]
                        br = [("biasall", h)]
                    else:
                        bap = biasc[:, 0:1]
                        br = [("biasc",)]
                    sop(lambda e, p=plo, b=bap, o=pre, r2=rot: e.activation(
                        numb[:, 512 * r2 + o:512 * r2 + CW], psB[:, p + o:p + CW],
                        AF.Exp, bias=b, scale=0.125),
                        reads=sk + br, writes=[("num", rot)], tag=f"ex{h}_{cch}_{t}")

                    def mk_attnv(t2, rot2, pre2, h2, cch2):
                        def emit():
                            for su in range(pre2 // 128, 4):
                                top(lambda e, su2=su, t3=t2, r2=rot2, h3=h2,
                                    st=(t2 == 4 * cch2 + su): e.matmul(
                                    psA[:, 512 * su2:512 * su2 + 65],
                                    numb[:, 512 * r2 + 128 * su2:512 * r2 + 128 * su2 + 128],
                                    vaug[:, t3 * VSTR + 65 * h3:t3 * VSTR + 65 * h3 + 65],
                                    start=(t3 == 0), stop=st),
                                    reads=[("num", rot2), ("vaug", t2), ("vone", t2)],
                                    writes=[("pA", su)], tag=f"av{h2}_{cch}_{t2}_{su}")
                        return emit
                    pend.append(mk_attnv(t, rot, pre, h, cch))
                    if len(pend) > 2:
                        pend.pop(0)()
                for fn in pend:
                    fn()
                rv = rden[:, (h % 2) * 4:(h % 2) * 4 + 4]
                den = psA[:].rearrange("p (s c) -> p s c", c=512)[:, :, 64:65]
                vop(lambda e, r2=rv, d=den: e.reciprocal(r2, d),
                    reads=[("pA", b) for b in range(4)], writes=[("rden", h % 2)],
                    tag=f"rcp{h}_{cch}")
                asrc = psA[:].rearrange("p (s c) -> p s c", c=512)[:, :, 0:64]
                adst = attok[:, cch * 4 * D:(cch + 1) * 4 * D].rearrange(
                    "p (s d) -> p s d", d=D)[:, :, 64 * h:64 * h + 64]
                rbc = rv.rearrange("p (s u) -> p s u", u=1).broadcast_to([128, 4, 64])
                vop(lambda e, d=adst, s2=asrc, r2=rbc: e.tensor_tensor(d, s2, r2, ALU.mult),
                    reads=[("pA", b) for b in range(4)] + [("rden", h % 2)],
                    writes=[("attok", cch, h)] + [("xnk", 4 * cch + j) for j in range(4)],
                    tag=f"aev{h}_{cch}")
                if _DEBUG and g == 0 and cch == 0 and s_ == 0:
                    dma("gpsimd", dbg["rden"], rden[:], "d_misc",
                        reads=[("rden", h % 2)], tag="dbgR")

    emit_wff_load()

    allqk = [("QTm", s) for s in range(GH)] + [("KTm", s) for s in range(GH)] + \
            [("QTrow", s) for s in range(GH)] + [("KTrow", s) for s in range(GH)]
    for s3 in range(NT):
        src = attok[:, s3 * D:(s3 + 1) * D]
        if s3 < 8:
            dstT = aS[:, 1024 * s3:1024 * s3 + 1024].rearrange(
                "p (b r) -> p b r", r=128)
            wk = [("attT", s3), ("xbar",),
                  ("stage",), ("hbsB",), ("swA",), ("swB",)]
        else:
            dstT = aQK[:, 1024 * (s3 - 8):1024 * (s3 - 8) + 1024].rearrange(
                "p (b r) -> p b r", r=128)
            wk = [("attT", s3), ("xbar",)] + (allqk if s3 == 8 else [])
        dma("sync", dstT, src, "d_sync",
            reads=[("attok", s3 // 4, h) for h in range(HEADS)] + [("xbar",)],
            writes=wk, tag=f"aT{s3}", transpose=True)

    if _DEBUG:
        dma("gpsimd", dbg["attok"], attok[:], "d_misc",
            reads=[("attok", c, h) for c in range(4) for h in range(HEADS)], tag="dbgA")
        dbg["attTa"] = nc.dram_tensor("dbg_attTa", [128, 8192], BF16, kind="ExternalOutput").ap()
        dbg["attTb"] = nc.dram_tensor("dbg_attTb", [128, 8192], BF16, kind="ExternalOutput").ap()
        dma("gpsimd", dbg["attTa"], aS[:, 0:8192], "d_misc",
            reads=[("attT", i) for i in range(8)], tag="dbgTa")
        dma("gpsimd", dbg["attTb"], aQK[:, 0:8192], "d_misc",
            reads=[("attT", i) for i in range(8, 16)], tag="dbgTb")
        dma("gpsimd", dbg["xnT"], xnT[:], "d_misc",
            reads=[("xnT", i) for i in range(NT)], tag="dbgX")
        dma("gpsimd", dbg["gate"], gate_all[:], "d_misc",
            reads=[("gate", i) for i in range(NT)], tag="dbgG")
        dma("gpsimd", dbg["tsb"], tsb[:], "d_misc",
            reads=[("tsb", i) for i in range(NT)], tag="dbgT")
        dma("gpsimd", dbg["vaug"], vaug[:], "d_misc",
            reads=[("vaug", i) for i in range(NT)] + [("vone", i) for i in range(NT)],
            tag="dbgV")

    def emit_tg(j):
        rj = j % 2
        tgvj = tg_b[:, rj * ER:rj * ER + ER]
        tgTj = tgT_b[:, rj * ER:rj * ER + ER]
        sj = rj * 1024
        for e2 in range(NE):
            vop(lambda e, v=tgvj, i2=j, e3=e2: e.tensor_scalar(
                v[:, RK * e3:RK * e3 + RK],
                tsb[:, i2 * ER + RK * e3:i2 * ER + RK * e3 + RK],
                gate_all[:, i2 * NE + e3:i2 * NE + e3 + 1], 1.0 / RK, ALU.mult, ALU.mult),
                reads=[("tsb", j), ("gate", j)],
                writes=[("tg", rj)], tag=f"tg{j}_{e2}")
        top(lambda e, v=tgvj, s2=sj: e.matmul(
            psA[:, s2:s2 + ER], v, identb[:], start=True, stop=True),
            reads=[("tg", rj), ("identb",)], writes=[("pA", sj // 512)],
            tag=f"tgM{j}")
        sop(lambda e, o=tgTj, s2=sj: e.activation(
            o, psA[:, s2:s2 + ER], AF.Copy, bias=0.0, scale=1.0),
            reads=[("pA", sj // 512)], writes=[("tgT", rj)],
            tag=f"tgE{j}")

    nstore = 0
    for i in range(NT):
        r = i % 2
        xrv = xr_f32[:, r * D:r * D + D]
        xr_extra = [("vaug", t) for t in range(4)] + [("vone", t) for t in range(4)] \
            if i < 2 else []
        dma("scalar", xrv, x_t[i], "d_x0" if i % 2 == 0 else "d_x1",
            reads=[("xt2", r)], writes=[("xrl", i)] + xr_extra, tag=f"xrl{i}")
        slo = r * 1024
        tgTv = tgT_b[:, r * ER:r * ER + ER]
        if i == 0:
            emit_tg(0)
        attTb = attT0 if i < 8 else attT1
        abase = 1024 * i if i < 8 else 1024 * (i - 8)
        for dt2 in range(DT):
            lhs = attTb[:, abase + 128 * dt2: abase + 128 * dt2 + 128]
            for c in range(2):
                top(lambda e, l=lhs, d2=dt2, c2=c, s2=slo: e.matmul(
                    psA[:, s2 + 512 * c2:s2 + 512 * c2 + 512], l,
                    wo_bf[:, d2 * D + 512 * c2:d2 * D + 512 * c2 + 512],
                    start=(d2 == 0), stop=(d2 == DT - 1)),
                    reads=[("attT", i), ("wo",)], writes=[("pA", slo // 512 + c)],
                    tag=f"wo{i}_{dt2}_{c}")
        if i + 1 < NT:
            emit_tg(i + 1)
        blo = r * 1024
        for dt2 in range(DT):
            lhs = xnT[:, dt2 * N + 128 * i: dt2 * N + 128 * i + 128]
            for c in range(2):
                top(lambda e, l=lhs, d2=dt2, c2=c, b=blo: e.matmul(
                    psB[:, b + 512 * c2:b + 512 * c2 + 512], l,
                    wff_bf[:, d2 * D + 512 * c2:d2 * D + 512 * c2 + 512],
                    start=(d2 == 0), stop=False),
                    reads=[("xnT", i), ("wff",)], writes=[("pB", blo // 512 + c)],
                    tag=f"ba{i}_{dt2}_{c}")
        for c in range(2):
            top(lambda e, v=tgTv, c2=c, b=blo: e.matmul(
                psB[:, b + 512 * c2:b + 512 * c2 + 512], v,
                wvl_bf[:, 512 * c2:512 * c2 + 512], start=False, stop=True),
                reads=[("tgT", r), ("wvl",)], writes=[("pB", blo // 512 + c)],
                tag=f"lo{i}_{c}")
        gv = gbuf0[:, 0:D] if r == 0 else gbuf1[:, 0:D]
        gk = ("wqkbuf",)
        sop(lambda e, gg=gv, b=blo: e.activation(gg, psB[:, b:b + D], AF.Silu,
                                                 bias=zeroc[:, 0:1], scale=1.0),
            reads=[("pB", blo // 512), ("pB", blo // 512 + 1), ("zeroc",)],
            writes=[("g", r), gk], tag=f"si{i}")
        vop(lambda e, gg=gv, s2=slo: e.tensor_tensor(gg, psA[:, s2:s2 + D], gg, ALU.mult),
            reads=[("pA", slo // 512), ("pA", slo // 512 + 1), ("g", r)],
            writes=[("g", r)], tag=f"sg{i}")
        vop(lambda e, o=xrv, gg=gv: e.tensor_tensor(o, gg, o, ALU.add),
            reads=[("g", r), ("xrl", i)], writes=[("outv", r)], tag=f"res{i}")
        dma("scalar", y_t[i], xrv, "d_scal", reads=[("outv", r)],
            writes=[("xt2", r)], tag=f"st{i}")
        nstore += 1

    dma_count = P.finalize()
    fin = [("sync", "d_scal", dma_count["d_scal"])]

    per_engine = {e: [rec for rec in P.ops if rec["engine"] == e] for e in ENGINES}

    with nc.Block() as block:
        def run_engine(recs, fins):
            def body(eng):
                for rec in recs:
                    for sn, v in rec["waits"]:
                        eng.wait_ge(sems[sn], v)
                    inst = rec["builder"](eng)
                    if rec["inc"] is not None:
                        sn, amt = rec["inc"]
                        inst.then_inc(sems[sn], amt)
                for sn, v in fins:
                    eng.wait_ge(sems[sn], v)
            return body
        block.tensor(run_engine(per_engine["tensor"], []))
        block.scalar(run_engine(per_engine["scalar"], []))
        block.vector(run_engine(per_engine["vector"], []))
        block.gpsimd(run_engine(per_engine["gpsimd"], []))
        block.sync(run_engine(per_engine["sync"],
                              [(sn, v) for (_, sn, v) in fin]))

    for cm in reversed(ctx):
        cm.__exit__(None, None, None)

    nops = {e: len(v) for e, v in per_engine.items()}
    return nc, nops


def kernel(L0_attn_x, L0_attn_w_qkv, L0_attn_w_o, L0_attn_log_xi,
           L0_ffn_weight, L0_ffn_gate_w, L0_ffn_lora_U, L0_ffn_lora_V):
    x = np.ascontiguousarray(np.asarray(L0_attn_x, dtype=np.float32))
    B = x.shape[0]
    wd = prep_weights(L0_attn_w_qkv, L0_attn_w_o, L0_attn_log_xi, L0_ffn_weight,
                      L0_ffn_gate_w, L0_ffn_lora_U, L0_ffn_lora_V)
    cd = build_consts()
    nc, _ = build()
    in_maps = []
    for b in range(B):
        m = {"x": np.ascontiguousarray(x[b])}
        m.update(wd)
        m["w_qk"] = wd["w_qk"]
        m["logxi"] = wd["logxi"]
        m.update(cd)
        in_maps.append(m)
    res = run_bass_kernel_spmd(nc, in_maps, core_ids=list(range(B)))
    out = np.stack([np.asarray(res.results[b]["y"]) for b in range(B)], axis=0)
    return out.astype(np.float32)


# version 28
# speedup vs baseline: 1.0141x; 1.0141x over previous
import numpy as np
import ml_dtypes
import concourse.bass as bass
import concourse.mybir as mybir
from concourse.bass_utils import run_bass_kernel_spmd

F32 = mybir.dt.float32
BF16 = mybir.dt.bfloat16
AF = mybir.ActivationFunctionType
ALU = mybir.AluOpType
AX = mybir.AxisListType

D = 1024
N = 2048
HEADS = 16
DH = 64
NT = 16
DT = 8
NE = 8
RK = 16
ER = NE * RK
M0 = 30.0
NEG = -1e30
GROUPS = 4
GH = 4
VSTR = HEADS * 65
CW = 512

ENGINES = ("tensor", "scalar", "vector", "gpsimd", "sync")

_DEBUG = False


def head_is_rope(h):
    return (h % 4) >= 2


def head_is_decay(h):
    return h % 2 == 1


def build_consts():
    c = {}
    pos = np.arange(N, dtype=np.float64)
    pm = pos - (N - 1)
    pm_hi = pm.astype(ml_dtypes.bfloat16).astype(np.float64)
    pm_lo = pm - pm_hi
    posm = np.zeros((65, 2 * N), dtype=np.float64)
    posm[64, 0:N] = pm_hi
    posm[64, N:2 * N] = pm_lo
    c["c_posm"] = posm.astype(ml_dtypes.bfloat16)
    p2 = np.arange(128)[:, None]
    t2 = np.arange(16)[None, :]
    c["c_iotam"] = ((128.0 * t2 + p2) - (N - 1)).astype(np.float32)
    u = np.arange(128)[None, :]
    c["c_masktri"] = np.where(u < np.arange(128)[:, None], NEG, 0.0).astype(
        ml_dtypes.bfloat16)
    inv_freq = 10000.0 ** (-(np.arange(0, DH, 2, dtype=np.float64) / DH))
    theta = pos[None, :] * inv_freq[:, None]
    cosv, sinv = np.cos(theta), np.sin(theta)
    c["c_trigC"] = np.concatenate([cosv, cosv], axis=0).astype(ml_dtypes.bfloat16)
    c["c_trigS"] = np.concatenate([-sinv, sinv], axis=0).astype(ml_dtypes.bfloat16)
    c["c_ident"] = np.eye(128).astype(ml_dtypes.bfloat16)
    return c


def perm_qk_cols():
    idx = np.zeros(D, dtype=np.int64)
    for h in range(HEADS):
        base = h * DH
        idx[base:base + 32] = base + np.arange(0, DH, 2)
        idx[base + 32:base + 64] = base + np.arange(1, DH, 2)
    return idx


def prep_weights(w_qkv, w_o, log_xi, w_ff, gate_w, lora_U, lora_V):
    idx = perm_qk_cols()
    wqkv = np.asarray(w_qkv, dtype=np.float32)
    d = {}
    d["w_qk"] = np.ascontiguousarray(
        np.concatenate([wqkv[:, 0:D][:, idx], wqkv[:, D:2 * D][:, idx]], axis=1))
    d["w_v"] = np.ascontiguousarray(wqkv[:, 2 * D:3 * D])
    d["w_o"] = np.ascontiguousarray(np.asarray(w_o, dtype=np.float32))
    d["w_ffT"] = np.ascontiguousarray(np.asarray(w_ff, dtype=np.float32).T)
    d["w_gT"] = np.ascontiguousarray(np.asarray(gate_w, dtype=np.float32).T)
    d["w_uT"] = np.ascontiguousarray(np.asarray(lora_U, dtype=np.float32).reshape(ER, D).T)
    d["w_vlT"] = np.ascontiguousarray(
        np.asarray(lora_V, dtype=np.float32).transpose(0, 2, 1).reshape(ER, D))
    d["logxi"] = np.ascontiguousarray(np.asarray(log_xi, dtype=np.float32).reshape(1, HEADS))
    return d


class Prog:

    def __init__(self):
        self.ops = []
        self.last_writer = {}
        self.readers = {}

    def op(self, engine, builder, reads=(), writes=(), dma_sem=None, tag=""):
        i = len(self.ops)
        raw = set()
        war = set()
        for r in reads:
            w = self.last_writer.get(r)
            if w is not None:
                raw.add(w)
        for r in writes:
            w = self.last_writer.get(r)
            if w is not None:
                raw.add(w)
            war.update(self.readers.get(r, ()))
        raw.discard(i)
        war.discard(i)
        self.ops.append(dict(i=i, engine=engine, builder=builder, raw=raw, war=war,
                             dma_sem=dma_sem, tag=tag, needed=False))
        for r in reads:
            self.readers.setdefault(r, []).append(i)
        for r in writes:
            self.last_writer[r] = i
            self.readers[r] = []
        return i

    def _need_edge(self, p, rec, is_raw):
        if p["dma_sem"] is not None:
            return True
        if p["engine"] != rec["engine"]:
            return True
        if p["engine"] == "tensor":
            return False
        return True

    def finalize(self):
        ops = self.ops
        for rec in ops:
            for d in rec["raw"]:
                if self._need_edge(ops[d], rec, True):
                    ops[d]["needed"] = True
            for d in rec["war"]:
                if self._need_edge(ops[d], rec, False):
                    ops[d]["needed"] = True
        eng_count = {e: 0 for e in ENGINES}
        dma_count = {}
        for rec in ops:
            if rec["dma_sem"] is not None:
                dma_count[rec["dma_sem"]] = dma_count.get(rec["dma_sem"], 0) + 16
                rec["ev"] = (rec["dma_sem"], dma_count[rec["dma_sem"]])
                rec["inc"] = (rec["dma_sem"], 16)
            elif rec["needed"]:
                eng_count[rec["engine"]] += 1
                rec["ev"] = ("e_" + rec["engine"], eng_count[rec["engine"]])
                rec["inc"] = ("e_" + rec["engine"], 1)
            else:
                rec["ev"] = None
                rec["inc"] = None
        hw = {e: {} for e in ENGINES}
        for rec in ops:
            e = rec["engine"]
            waits = {}
            for d, is_raw in [(d2, True) for d2 in sorted(rec["raw"])] + \
                             [(d2, False) for d2 in sorted(rec["war"])]:
                p = ops[d]
                if p["ev"] is None or not self._need_edge(p, rec, is_raw):
                    continue
                sn, v = p["ev"]
                if waits.get(sn, 0) < v:
                    waits[sn] = v
            wl = []
            for sn, v in sorted(waits.items()):
                if hw[e].get(sn, 0) < v:
                    hw[e][sn] = v
                    wl.append((sn, v))
            rec["waits"] = wl
        return dma_count


def build():
    nc = bass.Bass("TRN2", target_bir_lowering=False)

    x_hbm = nc.dram_tensor("x", [N, D], F32, kind="ExternalInput").ap()
    y_hbm = nc.dram_tensor("y", [N, D], F32, kind="ExternalOutput").ap()
    w_qk = nc.dram_tensor("w_qk", [D, 2 * D], F32, kind="ExternalInput").ap()
    w_v = nc.dram_tensor("w_v", [D, D], F32, kind="ExternalInput").ap()
    w_o = nc.dram_tensor("w_o", [D, D], F32, kind="ExternalInput").ap()
    w_ffT = nc.dram_tensor("w_ffT", [D, D], F32, kind="ExternalInput").ap()
    w_gT = nc.dram_tensor("w_gT", [D, NE], F32, kind="ExternalInput").ap()
    w_uT = nc.dram_tensor("w_uT", [D, ER], F32, kind="ExternalInput").ap()
    w_vlT = nc.dram_tensor("w_vlT", [ER, D], F32, kind="ExternalInput").ap()
    logxi_h = nc.dram_tensor("logxi", [1, HEADS], F32, kind="ExternalInput").ap()
    c_posm = nc.dram_tensor("c_posm", [65, 2 * N], BF16, kind="ExternalInput").ap()
    c_masktri = nc.dram_tensor("c_masktri", [128, 128], BF16, kind="ExternalInput").ap()
    c_iotam = nc.dram_tensor("c_iotam", [128, 16], F32, kind="ExternalInput").ap()
    c_trigC = nc.dram_tensor("c_trigC", [64, N], BF16, kind="ExternalInput").ap()
    c_trigS = nc.dram_tensor("c_trigS", [64, N], BF16, kind="ExternalInput").ap()
    c_ident = nc.dram_tensor("c_ident", [128, 128], BF16, kind="ExternalInput").ap()

    dbg = {}
    if _DEBUG:
        dbg["xnT"] = nc.dram_tensor("dbg_xnT", [128, DT * N], BF16, kind="ExternalOutput").ap()
        dbg["attok"] = nc.dram_tensor("dbg_attok", [128, NT * D], BF16, kind="ExternalOutput").ap()
        dbg["gate"] = nc.dram_tensor("dbg_gate", [128, NT * NE], F32, kind="ExternalOutput").ap()
        dbg["tsb"] = nc.dram_tensor("dbg_tsb", [128, NT * ER], BF16, kind="ExternalOutput").ap()
        dbg["vaug"] = nc.dram_tensor("dbg_vaug", [128, NT * VSTR], BF16, kind="ExternalOutput").ap()
        dbg["qt"] = nc.dram_tensor("dbg_qt", [128, 16384], BF16, kind="ExternalOutput").ap()
        dbg["num"] = nc.dram_tensor("dbg_num", [128, 1536], BF16, kind="ExternalOutput").ap()
        dbg["rden"] = nc.dram_tensor("dbg_rden", [128, 8], F32, kind="ExternalOutput").ap()

    x_t = x_hbm.rearrange("(n p) d -> n p d", p=128)
    y_t = y_hbm.rearrange("(n p) d -> n p d", p=128)

    ctx = []

    def sbuf(name, shape, dtype):
        cm = nc.sbuf_tensor(name, shape, dtype)
        t = cm.__enter__()
        ctx.append(cm)
        return t

    def psum_t(name, shape, dtype):
        cm = nc.psum_tensor(name, shape, dtype)
        t = cm.__enter__()
        ctx.append(cm)
        return t

    def semh(name):
        cm = nc.semaphore(name)
        s = cm.__enter__()
        ctx.append(cm)
        return s

    xnT = sbuf("xnT", [128, DT * N], BF16)
    vaug = sbuf("vaug", [128, NT * VSTR], BF16)
    attok = sbuf("attok", [128, NT * D], BF16)
    aQK = sbuf("aQK", [128, 16384], BF16)
    aW = sbuf("aW", [128, 16384], BF16)
    aS = sbuf("aS", [128, 14336], BF16)
    numb = sbuf("numb", [128, 2048], BF16)
    tsb = sbuf("tsb", [128, NT * ER], BF16)
    gate_all = sbuf("gate_all", [128, NT * NE], F32)
    gate_raw = sbuf("gate_raw", [128, NT * NE], F32)
    wg_bf = sbuf("wg_bf", [128, DT * NE], BF16)
    wu_bf = sbuf("wu_bf", [128, DT * ER], BF16)
    wvl_bf = sbuf("wvl_bf", [128, D], BF16)
    tg_b = sbuf("tg_b", [128, 2 * ER], BF16)
    tgT_b = sbuf("tgT_b", [128, 2 * ER], BF16)
    stats = sbuf("stats", [128, 12], F32)
    mvv = sbuf("mvv", [128, 16], F32)
    gsum = sbuf("gsum", [128, 32], F32)
    rden = sbuf("rden", [128, 8], F32)
    invxi = sbuf("invxi", [1, HEADS], F32)
    invxib = sbuf("invxib", [128, HEADS], F32)
    biasc = sbuf("biasc", [128, 1], F32)
    zeroc = sbuf("zeroc", [128, 1], F32)
    ones1 = sbuf("ones1", [1, 128], F32)
    masktri = sbuf("masktri", [128, 128], BF16)
    iotam = sbuf("iotam", [128, 16], F32)
    biasall = sbuf("biasall", [128, HEADS * 16], F32)
    identb = sbuf("identb", [128, 128], BF16)

    xt_f32 = aQK[:, 0:8192].bitcast(F32)
    xr_f32 = vaug[:, 0:4096].bitcast(F32)
    QT = [aQK[0:68, 2048 * s:2048 * (s + 1)] for s in range(GH)]
    KT = [aQK[0:68, 2048 * (4 + s):2048 * (5 + s)] for s in range(GH)]
    attT = aQK
    wv_bf = aW[:, 0:8192]
    trigC = aW[0:64, 8192:10240]
    trigS = aW[0:64, 10240:12288]
    posm = aW[0:65, 12288:16384]
    wo_bf = aW[:, 0:8192]
    wff_bf = aW[:, 8192:16384]
    stage = aS[:, 0:2048]
    hbsB = aS[0:64, 2048:4096]
    swapA = aS[0:64, 4096:6144]
    swapB = aS[0:64, 6144:8192]
    ropt1 = aS[0:64, 8192:10240]
    wqk_bf = aS[:, 10240:14336]
    gbuf0 = aS[:, 10240:12288].bitcast(F32)
    gbuf1 = aS[:, 12288:14336].bitcast(F32)
    attT0 = aS[:, 0:8192]
    attT1 = aQK[:, 0:8192]

    psA = psum_t("psA", [128, 2048], F32)
    psB = psum_t("psB", [128, 2048], F32)

    def kA(lo, hi):
        return [("pA", b) for b in range(lo // 512, (hi + 511) // 512)]

    def kB(lo, hi):
        return [("pB", b) for b in range(lo // 512, (hi + 511) // 512)]

    sems = {}
    for e in ENGINES:
        sems["e_" + e] = semh("e_" + e)
    for s_ in ("d_scal", "d_sync", "d_wqk", "d_wv0", "d_wv1", "d_wv2", "d_wv3",
               "d_wv4", "d_wv5", "d_wv6", "d_wv7",
               "d_wo", "d_wff", "d_wg",
               "d_wu", "d_wvl", "d_misc", "d_hb", "d_swA", "d_swB", "d_kb",
               "d_x0", "d_x1"):
        sems[s_] = semh(s_)

    P = Prog()

    def dma(engine, dst, src, dsem, reads=(), writes=(), tag="", transpose=False):
        if transpose:
            return P.op(engine, lambda e, d=dst, s=src: e.dma_start(d, s, transpose=True),
                        reads=reads, writes=writes, dma_sem=dsem, tag=tag)
        return P.op(engine, lambda e, d=dst, s=src: e.dma_start(d, s),
                    reads=reads, writes=writes, dma_sem=dsem, tag=tag)

    def vop(builder, reads=(), writes=(), tag=""):
        return P.op("vector", builder, reads, writes, tag=tag)

    def sop(builder, reads=(), writes=(), tag=""):
        return P.op("scalar", builder, reads, writes, tag=tag)

    def top(builder, reads=(), writes=(), tag=""):
        return P.op("tensor", builder, reads, writes, tag=tag)

    CONSTK = [("posm",), ("masktri",), ("ident",), ("iotam",), ("trigC",), ("trigS",), ("logxi",)]

    dma("gpsimd", posm, c_posm, "d_misc", writes=[("posm",)])
    dma("gpsimd", masktri[:], c_masktri, "d_misc", writes=[("masktri",)])
    dma("gpsimd", iotam[:], c_iotam, "d_misc", writes=[("iotam",)])
    dma("gpsimd", identb[:], c_ident, "d_misc", writes=[("ident",)])
    dma("gpsimd", trigC, c_trigC, "d_misc", writes=[("trigC",)])
    dma("gpsimd", trigS, c_trigS, "d_misc", writes=[("trigS",)])
    dma("gpsimd", invxi[:], logxi_h, "d_misc", writes=[("logxi",)])
    P.op("gpsimd", lambda e: e.engine_nop(), reads=list(CONSTK), writes=list(CONSTK),
         tag="constbar")

    wqk_r = w_qk.rearrange("(dt p) f -> p dt f", p=128)
    for dt2 in range(DT):
        dma("gpsimd", wg_bf[:, dt2 * NE:(dt2 + 1) * NE],
            w_gT.rearrange("(dt p) f -> dt p f", p=128)[dt2], "d_wg", writes=[("wg",)])
    wv_r = w_v.rearrange("(dt p) f -> p dt f", p=128)
    for dt2 in range(DT):
        dma("gpsimd", wv_bf[:, dt2 * D:(dt2 + 1) * D], wv_r[:, dt2],
            "d_wv%d" % dt2, writes=[("wv", dt2)])
    def emit_wqk_dmas(g):
        qs = 256 * g
        dstv = wqk_bf.rearrange("p (dt f) -> p dt f", dt=DT)
        dma("gpsimd", dstv[:, :, 0:256], wqk_r[:, :, qs:qs + 256], "d_wqk",
            writes=[("wqkbuf",)], tag=f"wqkQ{g}")
        dma("gpsimd", dstv[:, :, 256:512], wqk_r[:, :, D + qs:D + qs + 256], "d_wqk",
            writes=[("wqkbuf",)], tag=f"wqkK{g}")

    def emit_wqk_bar(g):
        P.op("vector", lambda e: e.engine_nop(), reads=[("wqkbuf",)],
             writes=[("wqkgrp", g)], tag=f"wqkbar{g}")

    emit_wqk_dmas(0)
    wu_r = w_uT.rearrange("(dt p) f -> p dt f", p=128)
    dma("gpsimd", wu_bf.rearrange("p (dt f) -> p dt f", dt=DT), wu_r, "d_wu",
        writes=[("wu",)])
    dma("gpsimd", wvl_bf[:], w_vlT, "d_wvl", writes=[("wvl",)])

    def emit_wo_load():
        dma("gpsimd", wo_bf.rearrange("p (dt f) -> p dt f", dt=DT),
            w_o.rearrange("(dt p) f -> p dt f", p=128), "d_wo",
            writes=[("wo",)] + [("wv", dt) for dt in range(DT)], tag="wo")

    def emit_wff_load():
        dma("gpsimd", wff_bf.rearrange("p (dt f) -> p dt f", dt=DT),
            w_ffT.rearrange("(dt p) f -> p dt f", p=128), "d_wff",
            writes=[("wff",), ("trigC",), ("trigS",), ("posm",)], tag="wff")

    vop(lambda e: e.memset(biasc[:], -M0), writes=[("biasc",)])
    vop(lambda e: e.memset(zeroc[:], 0.0), writes=[("zeroc",)])
    vop(lambda e: e.memset(ones1[:], 1.0), writes=[("ones1",)])
    def emit_qkrow_init():
        for s_ in range(GH):
            P.op("gpsimd", lambda e, s2=s_: e.memset(QT[s2][64:65, :], 8.0),
                 reads=[("xt", 0), ("xt", 1), ("xt", 2), ("xt", 3)],
                 writes=[("QTrow", s_)])
            P.op("gpsimd", lambda e, s2=s_: e.memset(KT[s2][64:65, :], 1.0),
                 reads=[("xt", 0), ("xt", 1), ("xt", 2), ("xt", 3)],
                 writes=[("KTrow", s_)])
    sop(lambda e: e.activation(invxi[:], invxi[:], AF.Exp, bias=zeroc[0:1, 0:1], scale=-1.0),
        reads=[("logxi",), ("zeroc",)], writes=[("invxi",)], tag="invxi")
    top(lambda e: e.matmul(psB[:, 0:16], ones1[:], invxi[:], start=True, stop=True),
        reads=[("ones1",), ("invxi",)], writes=kB(0, 16), tag="bcast")
    vop(lambda e: e.tensor_copy(invxib[:], psB[:, 0:16]),
        reads=kB(0, 16), writes=[("invxib",)])
    for h in range(HEADS):
        if head_is_decay(h):
            vop(lambda e, h2=h: e.tensor_scalar(
                biasall[:, 16 * h2:16 * h2 + 16], iotam[:],
                invxib[:, h2:h2 + 1], -M0, ALU.mult, ALU.add),
                reads=[("iotam",), ("invxib",)], writes=[("biasall", h)], tag=f"bias{h}")
    def emit_A(i):
        r = i % 4
        xv = xt_f32[:, r * D:r * D + D]
        dma("scalar", xv, x_t[i], "d_x0" if i % 2 == 0 else "d_x1",
            reads=[("xt", r)], writes=[("xld", i)], tag=f"xld{i}")
        mv = mvv[:, r * 4:r * 4 + 4]
        vop(lambda e, b=xv: e.bn_stats(stats[:, 0:6], b[:, 0:512]),
            reads=[("xld", i)], writes=[("st0",)], tag=f"bns{i}a")
        vop(lambda e, b=xv: e.bn_stats(stats[:, 6:12], b[:, 512:1024]),
            reads=[("xld", i)], writes=[("st1",)], tag=f"bns{i}b")
        vop(lambda e, m=mv: e.bn_aggr(m[:, 0:2], stats[:, 0:12]),
            reads=[("st0",), ("st1",)], writes=[("mv", r)], tag=f"bna{i}")
        vop(lambda e, m=mv: e.tensor_scalar_add(m[:, 2:3], m[:, 1:2], 1e-5),
            reads=[("mv", r)], writes=[("mv2", r)])
        sop(lambda e, m=mv: e.activation(m[:, 3:4], m[:, 2:3], AF.Ln,
                                         bias=zeroc[:, 0:1], scale=1.0),
            reads=[("mv2", r), ("zeroc",)], writes=[("mv3", r)], tag=f"ln{i}")
        sop(lambda e, m=mv: e.activation(m[:, 2:3], m[:, 3:4], AF.Exp,
                                         bias=zeroc[:, 0:1], scale=-0.5),
            reads=[("mv3", r), ("zeroc",)], writes=[("rstd", r)], tag=f"exr{i}")
        vop(lambda e, m=mv: e.tensor_scalar(
            m[:, 1:2], m[:, 0:1], m[:, 2:3], -1.0, ALU.mult, ALU.mult),
            reads=[("mv", r), ("rstd", r)], writes=[("mvb", r)], tag=f"nm{i}")

    def emit_B(i):
        r = i % 4
        xv = xt_f32[:, r * D:r * D + D]
        mv = mvv[:, r * 4:r * 4 + 4]
        xnv = attok[:, i * D:(i + 1) * D]
        sop(lambda e, m=mv, a=xnv, b=xv: e.activation(
            a, b, AF.Identity, bias=m[:, 1:2], scale=m[:, 2:3]),
            reads=[("xld", i), ("mvb", r), ("rstd", r)],
            writes=[("xnk", i), ("xt", r)], tag=f"xn{i}")
        dstT = xnT[:].rearrange("p (d n) -> p d n", n=N)[:, :, 128 * i:128 * i + 128]
        dma("sync", dstT, xnv, "d_sync", reads=[("xnk", i), ("xbar",)],
            writes=[("xnT", i), ("xbar",)], tag=f"xT{i}", transpose=True)

    def emit_C(i):
        r = i % 2
        vlo = r * 1024
        glo = 512 * (1 + r)
        for dt2 in range(DT):
            lhs = xnT[:, dt2 * N + 128 * i: dt2 * N + 128 * i + 128]
            for c in range(2):
                top(lambda e, l=lhs, d2=dt2, c2=c, v=vlo: e.matmul(
                    psA[:, v + 512 * c2:v + 512 * c2 + 512], l,
                    wv_bf[:, d2 * D + 512 * c2:d2 * D + 512 * c2 + 512],
                    start=(d2 == 0), stop=(d2 == DT - 1)),
                    reads=[("xnT", i), ("wv", dt2)], writes=[("pA", vlo // 512 + c)],
                    tag=f"v{i}_{dt2}_{c}")
        for dt2 in range(DT):
            lhs = xnT[:, dt2 * N + 128 * i: dt2 * N + 128 * i + 128]
            top(lambda e, l=lhs, d2=dt2, g2=glo: e.matmul(
                psB[:, g2:g2 + NE], l, wg_bf[:, d2 * NE:d2 * NE + NE],
                start=(d2 == 0), stop=(d2 == DT - 1)),
                reads=[("xnT", i), ("wg",)], writes=[("pB", glo // 512)], tag=f"g{i}_{dt2}")
        for dt2 in range(DT):
            lhs = xnT[:, dt2 * N + 128 * i: dt2 * N + 128 * i + 128]
            top(lambda e, l=lhs, d2=dt2, g2=glo: e.matmul(
                psB[:, g2 + NE:g2 + NE + ER], l, wu_bf[:, d2 * ER:d2 * ER + ER],
                start=(d2 == 0), stop=(d2 == DT - 1)),
                reads=[("xnT", i), ("wu",)], writes=[("pB", glo // 512)], tag=f"t{i}_{dt2}")

    def emit_E(i):
        r = i % 2
        vlo = r * 1024
        glo = 512 * (1 + r)
        vsrc = psA[:, vlo:vlo + 1024].rearrange("p (h v) -> p h v", v=64)
        vdst = vaug[:, i * VSTR:(i + 1) * VSTR].rearrange("p (h u) -> p h u", u=65)[:, :, 0:64]
        vop(lambda e, d=vdst, s=vsrc: e.tensor_copy(d, s),
            reads=[("pA", vlo // 512), ("pA", vlo // 512 + 1)],
            writes=[("vaug", i)], tag=f"vev{i}")
        vop(lambda e, i2=i, g2=glo: e.tensor_copy(tsb[:, i2 * ER:(i2 + 1) * ER],
                                                  psB[:, g2 + NE:g2 + NE + ER]),
            reads=[("pB", glo // 512)], writes=[("tsb", i)], tag=f"tev{i}")
        vop(lambda e, i2=i, g2=glo: e.tensor_copy(gate_raw[:, i2 * NE:(i2 + 1) * NE],
                                                  psB[:, g2:g2 + NE]),
            reads=[("pB", glo // 512)], writes=[("graw", i)], tag=f"gev{i}")

    for i in range(NT + 3):
        if i < NT:
            emit_A(i)
        if 1 <= i <= NT:
            emit_B(i - 1)
        if 3 <= i <= NT + 2:
            emit_C(i - 3)
            emit_E(i - 3)

    for t in range(NT):
        ap = vaug[:, t * VSTR:(t + 1) * VSTR].rearrange("p (h u) -> p h u", u=65)[:, :, 64:65]
        vop(lambda e, a=ap: e.memset(a, 1.0), writes=[("vone", t)], tag=f"vone{t}")

    sop(lambda e: e.activation(gate_all[:], gate_raw[:], AF.Exp, bias=zeroc[:, 0:1],
                               scale=1.0),
        reads=[("graw", i) for i in range(NT)] + [("zeroc",)], writes=[("gexp",)],
        tag="gexp")
    vop(lambda e: e.reduce_sum(gsum[:, 0:16],
                               gate_all[:].rearrange("p (t e) -> p t e", e=NE), AX.X),
        reads=[("gexp",)], writes=[("gsum",)], tag="gsum")
    vop(lambda e: e.reciprocal(gsum[:, 16:32], gsum[:, 0:16]),
        reads=[("gsum",)], writes=[("grec",)], tag="grec")
    vop(lambda e: e.tensor_tensor(
        gate_all[:].rearrange("p (t e) -> p t e", e=NE),
        gate_all[:].rearrange("p (t e) -> p t e", e=NE),
        gsum[:, 16:32].rearrange("p (t u) -> p t u", u=1).broadcast_to([128, NT, NE]),
        ALU.mult),
        reads=[("gexp",), ("grec",)], writes=[("gate", i) for i in range(NT)], tag="gfin")

    emit_wo_load()

    rot_counter = [0]
    emit_wqk_bar(0)
    emit_qkrow_init()
    for g in range(GROUPS):
        if g > 0:
            emit_wqk_dmas(g)
            emit_wqk_bar(g)
        for fi, ft in enumerate((0, 2, 1, 3)):
            isq = ft < 2
            ps, pk = (psA, [("pA", b) for b in range(4)]) if fi % 2 == 0 else \
                     (psB, [("pB", b) for b in range(4)])
            hA = 4 * g + (ft % 2) * 2
            hB = hA + 1
            slotA, slotB = hA - 4 * g, hB - 4 * g
            wcol = (0 if isq else 256) + (ft % 2) * 128
            for dt2 in range(DT):
                lhs = wqk_bf[:, dt2 * 512 + wcol: dt2 * 512 + wcol + 128]
                for c in range(4):
                    top(lambda e, p=ps, l=lhs, d2=dt2, c2=c: e.matmul(
                        p[:, 512 * c2:512 * c2 + 512], l,
                        xnT[:, d2 * N + 512 * c2: d2 * N + 512 * c2 + 512],
                        start=(d2 == 0), stop=(d2 == DT - 1)),
                        reads=[("wqkgrp", g), ("wqkbuf",)] + [("xnT", tt) for tt in range(NT)],
                        writes=pk, tag=f"qk{g}_{ft}_{dt2}_{c}")
            sop(lambda e, p=ps: e.copy(stage[:, 0:1024], p[:, 0:1024]),
                reads=pk, writes=[("stage",)], tag=f"stgA{g}_{ft}")
            vop(lambda e, p=ps: e.tensor_copy(stage[:, 1024:2048], p[:, 1024:2048]),
                reads=pk, writes=[("stage",)], tag=f"stgB{g}_{ft}")
            bufA = QT[slotA] if isq else KT[slotA]
            bufB = QT[slotB] if isq else KT[slotB]
            bkA = ("QTm" if isq else "KTm", slotA)
            bkB = ("QTm" if isq else "KTm", slotB)
            if head_is_rope(hA):
                dma("scalar", hbsB[0:64, :], stage[64:128, :], "d_hb",
                    reads=[("stage",)], writes=[("hbsB",)], tag=f"hb{g}_{ft}")
                dma("scalar", swapA[0:32, :], stage[32:64, :], "d_swA",
                    reads=[("stage",)], writes=[("swA",)], tag=f"sw1{g}_{ft}")
                dma("scalar", swapA[32:64, :], stage[0:32, :], "d_swA",
                    reads=[("stage",)], writes=[("swA",)], tag=f"sw2{g}_{ft}")
                dma("scalar", swapB[0:32, :], stage[96:128, :], "d_swB",
                    reads=[("stage",)], writes=[("swB",)], tag=f"sw3{g}_{ft}")
                dma("scalar", swapB[32:64, :], stage[64:96, :], "d_swB",
                    reads=[("stage",)], writes=[("swB",)], tag=f"sw4{g}_{ft}")
                for (base, swap, buf, bk, bask, swk) in (
                        (stage, swapA, bufA, bkA, ("stage",), ("swA",)),
                        (hbsB, swapB, bufB, bkB, ("hbsB",), ("swB",))):
                    vop(lambda e, b=buf, s=base: e.tensor_mul(
                        b[0:64, :], s[0:64, :], trigC[0:64, :]),
                        reads=[bask, ("trigC",)], writes=[bk], tag="rp_a")
                    vop(lambda e, s=swap, sk2=swk: e.tensor_mul(
                        ropt1[0:64, :], s[0:64, :], trigS[0:64, :]),
                        reads=[swk, ("trigS",)], writes=[("rt1",)], tag="rp_b")
                    vop(lambda e, b=buf: e.tensor_add(
                        b[0:64, :], ropt1[0:64, :], b[0:64, :]),
                        reads=[("rt1",), bk], writes=[bk], tag="rp_c")
            else:
                dma("scalar", bufB[0:64, :], stage[64:128, :], "d_hb",
                    reads=[("stage",)], writes=[bkB], tag=f"cpB{g}_{ft}")
                vop(lambda e, b=bufA: e.tensor_copy(b[0:64, :], stage[0:64, :]),
                    reads=[("stage",)], writes=[bkA], tag=f"cpA{g}_{ft}")
        for s_ in range(GH):
            h = 4 * g + s_
            if head_is_decay(h):
                vop(lambda e, s2=s_, h2=h: e.tensor_scalar(
                    QT[s2][64:65, :], posm[64:65, 0:N], invxib[64:65, h2:h2 + 1],
                    -8.0, ALU.mult, ALU.mult),
                    reads=[("posm",), ("invxib",), ("QTm", s_)],
                    writes=[("QTrow", s_)], tag=f"qrow{g}_{s_}")


        if _DEBUG and g == 0:
            dma("gpsimd", dbg["qt"], aQK[:], "d_misc",
                reads=[("QTm", s) for s in range(GH)] + [("KTm", s) for s in range(GH)]
                + [("QTrow", s) for s in range(GH)] + [("KTrow", s) for s in range(GH)],
                tag="dbgQ")

        for s_ in range(GH):
            for cch in range(4):
                h = 4 * g + s_
                decay = head_is_decay(h)
                qk_reads = [("QTm", s_), ("KTm", s_), ("QTrow", s_), ("KTrow", s_)]
                pend = []
                cs = 512 * cch
                for t in range(4 * cch + 4):
                    rot = rot_counter[0] % 3
                    rot_counter[0] += 1
                    plo = 512 * rot
                    sk = [("pB", rot)]
                    pre = max(0, 128 * t - cs)
                    top(lambda e, p=plo, t2=t, s2=s_, c2=cs: e.matmul(
                        psB[:, p:p + CW], KT[s2][0:65, 128 * t2:128 * t2 + 128],
                        QT[s2][0:65, c2:c2 + CW], start=True, stop=True),
                        reads=qk_reads, writes=sk, tag=f"sc{h}_{cch}_{t}")
                    if t >= 4 * cch:
                        vop(lambda e, p=plo, o=pre: e.tensor_add(
                            psB[:, p + o:p + o + 128], psB[:, p + o:p + o + 128],
                            masktri[:]),
                            reads=sk + [("masktri",)], writes=sk, tag=f"mk{h}_{cch}_{t}")
                    if decay:
                        bap = biasall[:, 16 * h + t:16 * h + t + 1]
                        br = [("biasall", h)]
                    else:
                        bap = biasc[:, 0:1]
                        br = [("biasc",)]
                    sop(lambda e, p=plo, b=bap, o=pre, r2=rot: e.activation(
                        numb[:, 512 * r2 + o:512 * r2 + CW], psB[:, p + o:p + CW],
                        AF.Exp, bias=b, scale=0.125),
                        reads=sk + br, writes=[("num", rot)], tag=f"ex{h}_{cch}_{t}")

                    def mk_attnv(t2, rot2, pre2, h2, cch2):
                        def emit():
                            for su in range(pre2 // 128, 4):
                                top(lambda e, su2=su, t3=t2, r2=rot2, h3=h2,
                                    st=(t2 == 4 * cch2 + su): e.matmul(
                                    psA[:, 512 * su2:512 * su2 + 65],
                                    numb[:, 512 * r2 + 128 * su2:512 * r2 + 128 * su2 + 128],
                                    vaug[:, t3 * VSTR + 65 * h3:t3 * VSTR + 65 * h3 + 65],
                                    start=(t3 == 0), stop=st),
                                    reads=[("num", rot2), ("vaug", t2), ("vone", t2)],
                                    writes=[("pA", su)], tag=f"av{h2}_{cch}_{t2}_{su}")
                        return emit
                    pend.append(mk_attnv(t, rot, pre, h, cch))
                    if len(pend) > 2:
                        pend.pop(0)()
                for fn in pend:
                    fn()
                rv = rden[:, (h % 2) * 4:(h % 2) * 4 + 4]
                den = psA[:].rearrange("p (s c) -> p s c", c=512)[:, :, 64:65]
                vop(lambda e, r2=rv, d=den: e.reciprocal(r2, d),
                    reads=[("pA", b) for b in range(4)], writes=[("rden", h % 2)],
                    tag=f"rcp{h}_{cch}")
                asrc = psA[:].rearrange("p (s c) -> p s c", c=512)[:, :, 0:64]
                adst = attok[:, cch * 4 * D:(cch + 1) * 4 * D].rearrange(
                    "p (s d) -> p s d", d=D)[:, :, 64 * h:64 * h + 64]
                rbc = rv.rearrange("p (s u) -> p s u", u=1).broadcast_to([128, 4, 64])
                vop(lambda e, d=adst, s2=asrc, r2=rbc: e.tensor_tensor(d, s2, r2, ALU.mult),
                    reads=[("pA", b) for b in range(4)] + [("rden", h % 2)],
                    writes=[("attok", cch, h)] + [("xnk", 4 * cch + j) for j in range(4)],
                    tag=f"aev{h}_{cch}")
                if _DEBUG and g == 0 and cch == 0 and s_ == 0:
                    dma("gpsimd", dbg["rden"], rden[:], "d_misc",
                        reads=[("rden", h % 2)], tag="dbgR")

    emit_wff_load()

    allqk = [("QTm", s) for s in range(GH)] + [("KTm", s) for s in range(GH)] + \
            [("QTrow", s) for s in range(GH)] + [("KTrow", s) for s in range(GH)]
    for s3 in range(NT):
        src = attok[:, s3 * D:(s3 + 1) * D]
        if s3 < 10:
            dstT = aS[:, 1024 * s3:1024 * s3 + 1024].rearrange(
                "p (b r) -> p b r", r=128)
            wk = [("attT", s3), ("xbar",),
                  ("stage",), ("hbsB",), ("swA",), ("swB",)] +                 ([("rt1",)] if s3 >= 8 else [])
        else:
            dstT = aQK[:, 1024 * (s3 - 10):1024 * (s3 - 10) + 1024].rearrange(
                "p (b r) -> p b r", r=128)
            wk = [("attT", s3), ("xbar",)] + (allqk if s3 == 10 else [])
        dma("sync", dstT, src, "d_sync",
            reads=[("attok", s3 // 4, h) for h in range(HEADS)] + [("xbar",)],
            writes=wk, tag=f"aT{s3}", transpose=True)

    if _DEBUG:
        dma("gpsimd", dbg["attok"], attok[:], "d_misc",
            reads=[("attok", c, h) for c in range(4) for h in range(HEADS)], tag="dbgA")
        dbg["attTa"] = nc.dram_tensor("dbg_attTa", [128, 8192], BF16, kind="ExternalOutput").ap()
        dbg["attTb"] = nc.dram_tensor("dbg_attTb", [128, 8192], BF16, kind="ExternalOutput").ap()
        dma("gpsimd", dbg["attTa"], aS[:, 0:8192], "d_misc",
            reads=[("attT", i) for i in range(8)], tag="dbgTa")
        dma("gpsimd", dbg["attTb"], aQK[:, 0:8192], "d_misc",
            reads=[("attT", i) for i in range(8, 16)], tag="dbgTb")
        dma("gpsimd", dbg["xnT"], xnT[:], "d_misc",
            reads=[("xnT", i) for i in range(NT)], tag="dbgX")
        dma("gpsimd", dbg["gate"], gate_all[:], "d_misc",
            reads=[("gate", i) for i in range(NT)], tag="dbgG")
        dma("gpsimd", dbg["tsb"], tsb[:], "d_misc",
            reads=[("tsb", i) for i in range(NT)], tag="dbgT")
        dma("gpsimd", dbg["vaug"], vaug[:], "d_misc",
            reads=[("vaug", i) for i in range(NT)] + [("vone", i) for i in range(NT)],
            tag="dbgV")

    def emit_tg(j):
        rj = j % 2
        tgvj = tg_b[:, rj * ER:rj * ER + ER]
        tgTj = tgT_b[:, rj * ER:rj * ER + ER]
        sj = rj * 1024
        for e2 in range(NE):
            vop(lambda e, v=tgvj, i2=j, e3=e2: e.tensor_scalar(
                v[:, RK * e3:RK * e3 + RK],
                tsb[:, i2 * ER + RK * e3:i2 * ER + RK * e3 + RK],
                gate_all[:, i2 * NE + e3:i2 * NE + e3 + 1], 1.0 / RK, ALU.mult, ALU.mult),
                reads=[("tsb", j), ("gate", j)],
                writes=[("tg", rj)], tag=f"tg{j}_{e2}")
        top(lambda e, v=tgvj, s2=sj: e.matmul(
            psA[:, s2:s2 + ER], v, identb[:], start=True, stop=True),
            reads=[("tg", rj), ("identb",)], writes=[("pA", sj // 512)],
            tag=f"tgM{j}")
        sop(lambda e, o=tgTj, s2=sj: e.activation(
            o, psA[:, s2:s2 + ER], AF.Copy, bias=0.0, scale=1.0),
            reads=[("pA", sj // 512)], writes=[("tgT", rj)],
            tag=f"tgE{j}")

    nstore = 0
    for i in range(NT):
        r = i % 2
        xrv = xr_f32[:, r * D:r * D + D]
        xr_extra = [("vaug", t) for t in range(4)] + [("vone", t) for t in range(4)] \
            if i < 2 else []
        dma("scalar", xrv, x_t[i], "d_x0" if i % 2 == 0 else "d_x1",
            reads=[("xt2", r)], writes=[("xrl", i)] + xr_extra, tag=f"xrl{i}")
        slo = r * 1024
        tgTv = tgT_b[:, r * ER:r * ER + ER]
        if i == 0:
            emit_tg(0)
        attTb = aS if i < 10 else attT1
        abase = 1024 * i if i < 10 else 1024 * (i - 10)
        for dt2 in range(DT):
            lhs = attTb[:, abase + 128 * dt2: abase + 128 * dt2 + 128]
            for c in range(2):
                top(lambda e, l=lhs, d2=dt2, c2=c, s2=slo: e.matmul(
                    psA[:, s2 + 512 * c2:s2 + 512 * c2 + 512], l,
                    wo_bf[:, d2 * D + 512 * c2:d2 * D + 512 * c2 + 512],
                    start=(d2 == 0), stop=(d2 == DT - 1)),
                    reads=[("attT", i), ("wo",)], writes=[("pA", slo // 512 + c)],
                    tag=f"wo{i}_{dt2}_{c}")
        if i + 1 < NT:
            emit_tg(i + 1)
        blo = r * 1024
        for dt2 in range(DT):
            lhs = xnT[:, dt2 * N + 128 * i: dt2 * N + 128 * i + 128]
            for c in range(2):
                top(lambda e, l=lhs, d2=dt2, c2=c, b=blo: e.matmul(
                    psB[:, b + 512 * c2:b + 512 * c2 + 512], l,
                    wff_bf[:, d2 * D + 512 * c2:d2 * D + 512 * c2 + 512],
                    start=(d2 == 0), stop=False),
                    reads=[("xnT", i), ("wff",)], writes=[("pB", blo // 512 + c)],
                    tag=f"ba{i}_{dt2}_{c}")
        for c in range(2):
            top(lambda e, v=tgTv, c2=c, b=blo: e.matmul(
                psB[:, b + 512 * c2:b + 512 * c2 + 512], v,
                wvl_bf[:, 512 * c2:512 * c2 + 512], start=False, stop=True),
                reads=[("tgT", r), ("wvl",)], writes=[("pB", blo // 512 + c)],
                tag=f"lo{i}_{c}")
        gv = gbuf0[:, 0:D] if r == 0 else gbuf1[:, 0:D]
        gk = ("wqkbuf",)
        sop(lambda e, gg=gv, b=blo: e.activation(gg, psB[:, b:b + D], AF.Silu,
                                                 bias=zeroc[:, 0:1], scale=1.0),
            reads=[("pB", blo // 512), ("pB", blo // 512 + 1), ("zeroc",)],
            writes=[("g", r), gk], tag=f"si{i}")
        vop(lambda e, gg=gv, s2=slo: e.tensor_tensor(gg, psA[:, s2:s2 + D], gg, ALU.mult),
            reads=[("pA", slo // 512), ("pA", slo // 512 + 1), ("g", r)],
            writes=[("g", r)], tag=f"sg{i}")
        vop(lambda e, o=xrv, gg=gv: e.tensor_tensor(o, gg, o, ALU.add),
            reads=[("g", r), ("xrl", i)], writes=[("outv", r)], tag=f"res{i}")
        dma("scalar", y_t[i], xrv, "d_scal", reads=[("outv", r)],
            writes=[("xt2", r)], tag=f"st{i}")
        nstore += 1

    dma_count = P.finalize()
    fin = [("sync", "d_scal", dma_count["d_scal"])]

    per_engine = {e: [rec for rec in P.ops if rec["engine"] == e] for e in ENGINES}

    with nc.Block() as block:
        def run_engine(recs, fins):
            def body(eng):
                for rec in recs:
                    for sn, v in rec["waits"]:
                        eng.wait_ge(sems[sn], v)
                    inst = rec["builder"](eng)
                    if rec["inc"] is not None:
                        sn, amt = rec["inc"]
                        inst.then_inc(sems[sn], amt)
                for sn, v in fins:
                    eng.wait_ge(sems[sn], v)
            return body
        block.tensor(run_engine(per_engine["tensor"], []))
        block.scalar(run_engine(per_engine["scalar"], []))
        block.vector(run_engine(per_engine["vector"], []))
        block.gpsimd(run_engine(per_engine["gpsimd"], []))
        block.sync(run_engine(per_engine["sync"],
                              [(sn, v) for (_, sn, v) in fin]))

    for cm in reversed(ctx):
        cm.__exit__(None, None, None)

    nops = {e: len(v) for e, v in per_engine.items()}
    return nc, nops


def kernel(L0_attn_x, L0_attn_w_qkv, L0_attn_w_o, L0_attn_log_xi,
           L0_ffn_weight, L0_ffn_gate_w, L0_ffn_lora_U, L0_ffn_lora_V):
    x = np.ascontiguousarray(np.asarray(L0_attn_x, dtype=np.float32))
    B = x.shape[0]
    wd = prep_weights(L0_attn_w_qkv, L0_attn_w_o, L0_attn_log_xi, L0_ffn_weight,
                      L0_ffn_gate_w, L0_ffn_lora_U, L0_ffn_lora_V)
    cd = build_consts()
    nc, _ = build()
    in_maps = []
    for b in range(B):
        m = {"x": np.ascontiguousarray(x[b])}
        m.update(wd)
        m["w_qk"] = wd["w_qk"]
        m["logxi"] = wd["logxi"]
        m.update(cd)
        in_maps.append(m)
    res = run_bass_kernel_spmd(nc, in_maps, core_ids=list(range(B)))
    out = np.stack([np.asarray(res.results[b]["y"]) for b in range(B)], axis=0)
    return out.astype(np.float32)


# version 29 (final)
# speedup vs baseline: 1.0240x; 1.0098x over previous
import numpy as np
import ml_dtypes
import concourse.bass as bass
import concourse.mybir as mybir
from concourse.bass_utils import run_bass_kernel_spmd

F32 = mybir.dt.float32
BF16 = mybir.dt.bfloat16
AF = mybir.ActivationFunctionType
ALU = mybir.AluOpType
AX = mybir.AxisListType

D = 1024
N = 2048
HEADS = 16
DH = 64
NT = 16
DT = 8
NE = 8
RK = 16
ER = NE * RK
M0 = 30.0
NEG = -1e30
GROUPS = 4
GH = 4
VSTR = HEADS * 65
CW = 512

ENGINES = ("tensor", "scalar", "vector", "gpsimd", "sync")

_DEBUG = False


def head_is_rope(h):
    return (h % 4) >= 2


def head_is_decay(h):
    return h % 2 == 1


def build_consts():
    c = {}
    pos = np.arange(N, dtype=np.float64)
    pm = pos - (N - 1)
    pm_hi = pm.astype(ml_dtypes.bfloat16).astype(np.float64)
    pm_lo = pm - pm_hi
    posm = np.zeros((65, 2 * N), dtype=np.float64)
    posm[64, 0:N] = pm_hi
    posm[64, N:2 * N] = pm_lo
    c["c_posm"] = posm.astype(ml_dtypes.bfloat16)
    p2 = np.arange(128)[:, None]
    t2 = np.arange(16)[None, :]
    c["c_iotam"] = ((128.0 * t2 + p2) - (N - 1)).astype(np.float32)
    u = np.arange(128)[None, :]
    c["c_masktri"] = np.where(u < np.arange(128)[:, None], NEG, 0.0).astype(
        ml_dtypes.bfloat16)
    inv_freq = 10000.0 ** (-(np.arange(0, DH, 2, dtype=np.float64) / DH))
    theta = pos[None, :] * inv_freq[:, None]
    cosv, sinv = np.cos(theta), np.sin(theta)
    c["c_trigC"] = np.concatenate([cosv, cosv], axis=0).astype(ml_dtypes.bfloat16)
    c["c_trigS"] = np.concatenate([-sinv, sinv], axis=0).astype(ml_dtypes.bfloat16)
    c["c_ident"] = np.eye(128).astype(ml_dtypes.bfloat16)
    return c


def perm_qk_cols():
    idx = np.zeros(D, dtype=np.int64)
    for h in range(HEADS):
        base = h * DH
        idx[base:base + 32] = base + np.arange(0, DH, 2)
        idx[base + 32:base + 64] = base + np.arange(1, DH, 2)
    return idx


def prep_weights(w_qkv, w_o, log_xi, w_ff, gate_w, lora_U, lora_V):
    idx = perm_qk_cols()
    wqkv = np.asarray(w_qkv, dtype=np.float32)
    d = {}
    d["w_qk"] = np.ascontiguousarray(
        np.concatenate([wqkv[:, 0:D][:, idx], wqkv[:, D:2 * D][:, idx]], axis=1))
    d["w_v"] = np.ascontiguousarray(wqkv[:, 2 * D:3 * D])
    d["w_o"] = np.ascontiguousarray(np.asarray(w_o, dtype=np.float32))
    d["w_ffT"] = np.ascontiguousarray(np.asarray(w_ff, dtype=np.float32).T)
    d["w_gT"] = np.ascontiguousarray(np.asarray(gate_w, dtype=np.float32).T)
    d["w_uT"] = np.ascontiguousarray(np.asarray(lora_U, dtype=np.float32).reshape(ER, D).T)
    d["w_vlT"] = np.ascontiguousarray(
        np.asarray(lora_V, dtype=np.float32).transpose(0, 2, 1).reshape(ER, D))
    d["logxi"] = np.ascontiguousarray(np.asarray(log_xi, dtype=np.float32).reshape(1, HEADS))
    return d


class Prog:

    def __init__(self):
        self.ops = []
        self.last_writer = {}
        self.readers = {}

    def op(self, engine, builder, reads=(), writes=(), dma_sem=None, tag=""):
        i = len(self.ops)
        raw = set()
        war = set()
        for r in reads:
            w = self.last_writer.get(r)
            if w is not None:
                raw.add(w)
        for r in writes:
            w = self.last_writer.get(r)
            if w is not None:
                raw.add(w)
            war.update(self.readers.get(r, ()))
        raw.discard(i)
        war.discard(i)
        self.ops.append(dict(i=i, engine=engine, builder=builder, raw=raw, war=war,
                             dma_sem=dma_sem, tag=tag, needed=False))
        for r in reads:
            self.readers.setdefault(r, []).append(i)
        for r in writes:
            self.last_writer[r] = i
            self.readers[r] = []
        return i

    def _need_edge(self, p, rec, is_raw):
        if p["dma_sem"] is not None:
            return True
        if p["engine"] != rec["engine"]:
            return True
        if p["engine"] == "tensor":
            return False
        return True

    def finalize(self):
        ops = self.ops
        for rec in ops:
            for d in rec["raw"]:
                if self._need_edge(ops[d], rec, True):
                    ops[d]["needed"] = True
            for d in rec["war"]:
                if self._need_edge(ops[d], rec, False):
                    ops[d]["needed"] = True
        eng_count = {e: 0 for e in ENGINES}
        dma_count = {}
        for rec in ops:
            if rec["dma_sem"] is not None:
                dma_count[rec["dma_sem"]] = dma_count.get(rec["dma_sem"], 0) + 16
                rec["ev"] = (rec["dma_sem"], dma_count[rec["dma_sem"]])
                rec["inc"] = (rec["dma_sem"], 16)
            elif rec["needed"]:
                eng_count[rec["engine"]] += 1
                rec["ev"] = ("e_" + rec["engine"], eng_count[rec["engine"]])
                rec["inc"] = ("e_" + rec["engine"], 1)
            else:
                rec["ev"] = None
                rec["inc"] = None
        hw = {e: {} for e in ENGINES}
        for rec in ops:
            e = rec["engine"]
            waits = {}
            for d, is_raw in [(d2, True) for d2 in sorted(rec["raw"])] + \
                             [(d2, False) for d2 in sorted(rec["war"])]:
                p = ops[d]
                if p["ev"] is None or not self._need_edge(p, rec, is_raw):
                    continue
                sn, v = p["ev"]
                if waits.get(sn, 0) < v:
                    waits[sn] = v
            wl = []
            for sn, v in sorted(waits.items()):
                if hw[e].get(sn, 0) < v:
                    hw[e][sn] = v
                    wl.append((sn, v))
            rec["waits"] = wl
        return dma_count


def build():
    nc = bass.Bass("TRN2", target_bir_lowering=False)

    x_hbm = nc.dram_tensor("x", [N, D], F32, kind="ExternalInput").ap()
    y_hbm = nc.dram_tensor("y", [N, D], F32, kind="ExternalOutput").ap()
    w_qk = nc.dram_tensor("w_qk", [D, 2 * D], F32, kind="ExternalInput").ap()
    w_v = nc.dram_tensor("w_v", [D, D], F32, kind="ExternalInput").ap()
    w_o = nc.dram_tensor("w_o", [D, D], F32, kind="ExternalInput").ap()
    w_ffT = nc.dram_tensor("w_ffT", [D, D], F32, kind="ExternalInput").ap()
    w_gT = nc.dram_tensor("w_gT", [D, NE], F32, kind="ExternalInput").ap()
    w_uT = nc.dram_tensor("w_uT", [D, ER], F32, kind="ExternalInput").ap()
    w_vlT = nc.dram_tensor("w_vlT", [ER, D], F32, kind="ExternalInput").ap()
    logxi_h = nc.dram_tensor("logxi", [1, HEADS], F32, kind="ExternalInput").ap()
    c_posm = nc.dram_tensor("c_posm", [65, 2 * N], BF16, kind="ExternalInput").ap()
    c_masktri = nc.dram_tensor("c_masktri", [128, 128], BF16, kind="ExternalInput").ap()
    c_iotam = nc.dram_tensor("c_iotam", [128, 16], F32, kind="ExternalInput").ap()
    c_trigC = nc.dram_tensor("c_trigC", [64, N], BF16, kind="ExternalInput").ap()
    c_trigS = nc.dram_tensor("c_trigS", [64, N], BF16, kind="ExternalInput").ap()
    c_ident = nc.dram_tensor("c_ident", [128, 128], BF16, kind="ExternalInput").ap()

    dbg = {}
    if _DEBUG:
        dbg["xnT"] = nc.dram_tensor("dbg_xnT", [128, DT * N], BF16, kind="ExternalOutput").ap()
        dbg["attok"] = nc.dram_tensor("dbg_attok", [128, NT * D], BF16, kind="ExternalOutput").ap()
        dbg["gate"] = nc.dram_tensor("dbg_gate", [128, NT * NE], F32, kind="ExternalOutput").ap()
        dbg["tsb"] = nc.dram_tensor("dbg_tsb", [128, NT * ER], BF16, kind="ExternalOutput").ap()
        dbg["vaug"] = nc.dram_tensor("dbg_vaug", [128, NT * VSTR], BF16, kind="ExternalOutput").ap()
        dbg["qt"] = nc.dram_tensor("dbg_qt", [128, 16384], BF16, kind="ExternalOutput").ap()
        dbg["num"] = nc.dram_tensor("dbg_num", [128, 1536], BF16, kind="ExternalOutput").ap()
        dbg["rden"] = nc.dram_tensor("dbg_rden", [128, 8], F32, kind="ExternalOutput").ap()

    x_t = x_hbm.rearrange("(n p) d -> n p d", p=128)
    y_t = y_hbm.rearrange("(n p) d -> n p d", p=128)

    ctx = []

    def sbuf(name, shape, dtype):
        cm = nc.sbuf_tensor(name, shape, dtype)
        t = cm.__enter__()
        ctx.append(cm)
        return t

    def psum_t(name, shape, dtype):
        cm = nc.psum_tensor(name, shape, dtype)
        t = cm.__enter__()
        ctx.append(cm)
        return t

    def semh(name):
        cm = nc.semaphore(name)
        s = cm.__enter__()
        ctx.append(cm)
        return s

    xnT = sbuf("xnT", [128, DT * N], BF16)
    vaug = sbuf("vaug", [128, NT * VSTR], BF16)
    attok = sbuf("attok", [128, NT * D], BF16)
    aQK = sbuf("aQK", [128, 16384], BF16)
    aW = sbuf("aW", [128, 16384], BF16)
    aS = sbuf("aS", [128, 14336], BF16)
    numb = sbuf("numb", [128, 2048], BF16)
    tsb = sbuf("tsb", [128, NT * ER], BF16)
    gate_all = sbuf("gate_all", [128, NT * NE], F32)
    gate_raw = sbuf("gate_raw", [128, NT * NE], F32)
    wg_bf = sbuf("wg_bf", [128, DT * NE], BF16)
    wu_bf = sbuf("wu_bf", [128, DT * ER], BF16)
    wvl_bf = sbuf("wvl_bf", [128, D], BF16)
    tg_b = sbuf("tg_b", [128, 2 * ER], BF16)
    tgT_b = sbuf("tgT_b", [128, 2 * ER], BF16)
    stats = sbuf("stats", [128, 12], F32)
    mvv = sbuf("mvv", [128, 16], F32)
    gsum = sbuf("gsum", [128, 32], F32)
    rden = sbuf("rden", [128, 8], F32)
    invxi = sbuf("invxi", [1, HEADS], F32)
    invxib = sbuf("invxib", [128, HEADS], F32)
    biasc = sbuf("biasc", [128, 1], F32)
    zeroc = sbuf("zeroc", [128, 1], F32)
    ones1 = sbuf("ones1", [1, 128], F32)
    masktri = sbuf("masktri", [128, 128], BF16)
    iotam = sbuf("iotam", [128, 16], F32)
    biasall = sbuf("biasall", [128, HEADS * 16], F32)
    identb = sbuf("identb", [128, 128], BF16)

    xt_f32 = aQK[:, 0:8192].bitcast(F32)
    xr_f32 = vaug[:, 0:4096].bitcast(F32)
    QT = [aQK[0:68, 2048 * s:2048 * (s + 1)] for s in range(GH)]
    KT = [aQK[0:68, 2048 * (4 + s):2048 * (5 + s)] for s in range(GH)]
    attT = aQK
    wv_bf = aW[:, 0:8192]
    trigC = aW[0:64, 8192:10240]
    trigS = aW[0:64, 10240:12288]
    posm = aW[0:65, 12288:16384]
    wo_bf = aW[:, 0:8192]
    wff_bf = aW[:, 8192:16384]
    stage = aS[:, 0:2048]
    hbsB = aS[0:64, 2048:4096]
    swapA = aS[0:64, 4096:6144]
    swapB = aS[0:64, 6144:8192]
    ropt1 = aS[0:64, 8192:10240]
    wqk_bf = aS[:, 10240:14336]
    gbuf0 = aS[:, 10240:12288].bitcast(F32)
    gbuf1 = aS[:, 12288:14336].bitcast(F32)
    attT0 = aS[:, 0:8192]
    attT1 = aQK[:, 0:8192]

    psA = psum_t("psA", [128, 2048], F32)
    psB = psum_t("psB", [128, 2048], F32)

    def kA(lo, hi):
        return [("pA", b) for b in range(lo // 512, (hi + 511) // 512)]

    def kB(lo, hi):
        return [("pB", b) for b in range(lo // 512, (hi + 511) // 512)]

    sems = {}
    for e in ENGINES:
        sems["e_" + e] = semh("e_" + e)
    for s_ in ("d_scal", "d_sync", "d_wqk", "d_wv0", "d_wv1", "d_wv2", "d_wv3",
               "d_wv4", "d_wv5", "d_wv6", "d_wv7",
               "d_wo", "d_wff", "d_wg",
               "d_wu", "d_wvl", "d_misc", "d_hb", "d_swA", "d_swB", "d_kb",
               "d_x0", "d_x1"):
        sems[s_] = semh(s_)

    P = Prog()

    def dma(engine, dst, src, dsem, reads=(), writes=(), tag="", transpose=False):
        if transpose:
            return P.op(engine, lambda e, d=dst, s=src: e.dma_start(d, s, transpose=True),
                        reads=reads, writes=writes, dma_sem=dsem, tag=tag)
        return P.op(engine, lambda e, d=dst, s=src: e.dma_start(d, s),
                    reads=reads, writes=writes, dma_sem=dsem, tag=tag)

    def vop(builder, reads=(), writes=(), tag=""):
        return P.op("vector", builder, reads, writes, tag=tag)

    def sop(builder, reads=(), writes=(), tag=""):
        return P.op("scalar", builder, reads, writes, tag=tag)

    def top(builder, reads=(), writes=(), tag=""):
        return P.op("tensor", builder, reads, writes, tag=tag)

    CONSTK = [("posm",), ("masktri",), ("ident",), ("iotam",), ("trigC",), ("trigS",), ("logxi",)]

    dma("gpsimd", posm, c_posm, "d_misc", writes=[("posm",)])
    dma("gpsimd", masktri[:], c_masktri, "d_misc", writes=[("masktri",)])
    dma("gpsimd", iotam[:], c_iotam, "d_misc", writes=[("iotam",)])
    dma("gpsimd", identb[:], c_ident, "d_misc", writes=[("ident",)])
    dma("gpsimd", trigC, c_trigC, "d_misc", writes=[("trigC",)])
    dma("gpsimd", trigS, c_trigS, "d_misc", writes=[("trigS",)])
    dma("gpsimd", invxi[:], logxi_h, "d_misc", writes=[("logxi",)])
    P.op("gpsimd", lambda e: e.engine_nop(), reads=list(CONSTK), writes=list(CONSTK),
         tag="constbar")

    wqk_r = w_qk.rearrange("(dt p) f -> p dt f", p=128)
    for dt2 in range(DT):
        dma("gpsimd", wg_bf[:, dt2 * NE:(dt2 + 1) * NE],
            w_gT.rearrange("(dt p) f -> dt p f", p=128)[dt2], "d_wg", writes=[("wg",)])
    wv_r = w_v.rearrange("(dt p) f -> p dt f", p=128)
    for dt2 in range(DT):
        dma("gpsimd", wv_bf[:, dt2 * D:(dt2 + 1) * D], wv_r[:, dt2],
            "d_wv%d" % dt2, writes=[("wv", dt2)])
    def emit_wqk_dmas(g):
        qs = 256 * g
        dstv = wqk_bf.rearrange("p (dt f) -> p dt f", dt=DT)
        dma("gpsimd", dstv[:, :, 0:256], wqk_r[:, :, qs:qs + 256], "d_wqk",
            writes=[("wqkbuf",)], tag=f"wqkQ{g}")
        dma("gpsimd", dstv[:, :, 256:512], wqk_r[:, :, D + qs:D + qs + 256], "d_wqk",
            writes=[("wqkbuf",)], tag=f"wqkK{g}")

    def emit_wqk_bar(g):
        P.op("vector", lambda e: e.engine_nop(), reads=[("wqkbuf",)],
             writes=[("wqkgrp", g)], tag=f"wqkbar{g}")

    emit_wqk_dmas(0)
    wu_r = w_uT.rearrange("(dt p) f -> p dt f", p=128)
    dma("gpsimd", wu_bf.rearrange("p (dt f) -> p dt f", dt=DT), wu_r, "d_wu",
        writes=[("wu",)])
    dma("gpsimd", wvl_bf[:], w_vlT, "d_wvl", writes=[("wvl",)])

    def emit_wo_load():
        dma("gpsimd", wo_bf.rearrange("p (dt f) -> p dt f", dt=DT),
            w_o.rearrange("(dt p) f -> p dt f", p=128), "d_wo",
            writes=[("wo",)] + [("wv", dt) for dt in range(DT)], tag="wo")

    def emit_wff_load():
        dma("gpsimd", wff_bf.rearrange("p (dt f) -> p dt f", dt=DT),
            w_ffT.rearrange("(dt p) f -> p dt f", p=128), "d_wff",
            writes=[("wff",), ("trigC",), ("trigS",), ("posm",)], tag="wff")

    vop(lambda e: e.memset(biasc[:], -M0), writes=[("biasc",)])
    vop(lambda e: e.memset(zeroc[:], 0.0), writes=[("zeroc",)])
    vop(lambda e: e.memset(ones1[:], 1.0), writes=[("ones1",)])
    def emit_qkrow_init():
        for s_ in range(GH):
            P.op("gpsimd", lambda e, s2=s_: e.memset(QT[s2][64:65, :], 8.0),
                 reads=[("xt", 0), ("xt", 1), ("xt", 2), ("xt", 3)],
                 writes=[("QTrow", s_)])
            P.op("gpsimd", lambda e, s2=s_: e.memset(KT[s2][64:65, :], 1.0),
                 reads=[("xt", 0), ("xt", 1), ("xt", 2), ("xt", 3)],
                 writes=[("KTrow", s_)])
    sop(lambda e: e.activation(invxi[:], invxi[:], AF.Exp, bias=zeroc[0:1, 0:1], scale=-1.0),
        reads=[("logxi",), ("zeroc",)], writes=[("invxi",)], tag="invxi")
    top(lambda e: e.matmul(psB[:, 0:16], ones1[:], invxi[:], start=True, stop=True),
        reads=[("ones1",), ("invxi",)], writes=kB(0, 16), tag="bcast")
    vop(lambda e: e.tensor_copy(invxib[:], psB[:, 0:16]),
        reads=kB(0, 16), writes=[("invxib",)])
    for h in range(HEADS):
        if head_is_decay(h):
            vop(lambda e, h2=h: e.tensor_scalar(
                biasall[:, 16 * h2:16 * h2 + 16], iotam[:],
                invxib[:, h2:h2 + 1], -M0, ALU.mult, ALU.add),
                reads=[("iotam",), ("invxib",)], writes=[("biasall", h)], tag=f"bias{h}")
    def emit_A(i):
        r = i % 4
        xv = xt_f32[:, r * D:r * D + D]
        dma("scalar", xv, x_t[i], "d_x0" if i % 2 == 0 else "d_x1",
            reads=[("xt", r)], writes=[("xld", i)], tag=f"xld{i}")
        mv = mvv[:, r * 4:r * 4 + 4]
        vop(lambda e, b=xv: e.bn_stats(stats[:, 0:6], b[:, 0:512]),
            reads=[("xld", i)], writes=[("st0",)], tag=f"bns{i}a")
        vop(lambda e, b=xv: e.bn_stats(stats[:, 6:12], b[:, 512:1024]),
            reads=[("xld", i)], writes=[("st1",)], tag=f"bns{i}b")
        vop(lambda e, m=mv: e.bn_aggr(m[:, 0:2], stats[:, 0:12]),
            reads=[("st0",), ("st1",)], writes=[("mv", r)], tag=f"bna{i}")
        vop(lambda e, m=mv: e.tensor_scalar_add(m[:, 2:3], m[:, 1:2], 1e-5),
            reads=[("mv", r)], writes=[("mv2", r)])
        sop(lambda e, m=mv: e.activation(m[:, 3:4], m[:, 2:3], AF.Ln,
                                         bias=zeroc[:, 0:1], scale=1.0),
            reads=[("mv2", r), ("zeroc",)], writes=[("mv3", r)], tag=f"ln{i}")
        sop(lambda e, m=mv: e.activation(m[:, 2:3], m[:, 3:4], AF.Exp,
                                         bias=zeroc[:, 0:1], scale=-0.5),
            reads=[("mv3", r), ("zeroc",)], writes=[("rstd", r)], tag=f"exr{i}")
        vop(lambda e, m=mv: e.tensor_scalar(
            m[:, 1:2], m[:, 0:1], m[:, 2:3], -1.0, ALU.mult, ALU.mult),
            reads=[("mv", r), ("rstd", r)], writes=[("mvb", r)], tag=f"nm{i}")

    def emit_B(i):
        r = i % 4
        xv = xt_f32[:, r * D:r * D + D]
        mv = mvv[:, r * 4:r * 4 + 4]
        xnv = attok[:, i * D:(i + 1) * D]
        sop(lambda e, m=mv, a=xnv, b=xv: e.activation(
            a, b, AF.Identity, bias=m[:, 1:2], scale=m[:, 2:3]),
            reads=[("xld", i), ("mvb", r), ("rstd", r)],
            writes=[("xnk", i), ("xt", r)], tag=f"xn{i}")
        dstT = xnT[:].rearrange("p (d n) -> p d n", n=N)[:, :, 128 * i:128 * i + 128]
        dma("sync", dstT, xnv, "d_sync", reads=[("xnk", i), ("xbar",)],
            writes=[("xnT", i), ("xbar",)], tag=f"xT{i}", transpose=True)

    def emit_C(i):
        r = i % 2
        vlo = r * 1024
        glo = 512 * (1 + r)
        for dt2 in range(DT):
            lhs = xnT[:, dt2 * N + 128 * i: dt2 * N + 128 * i + 128]
            for c in range(2):
                top(lambda e, l=lhs, d2=dt2, c2=c, v=vlo: e.matmul(
                    psA[:, v + 512 * c2:v + 512 * c2 + 512], l,
                    wv_bf[:, d2 * D + 512 * c2:d2 * D + 512 * c2 + 512],
                    start=(d2 == 0), stop=(d2 == DT - 1)),
                    reads=[("xnT", i), ("wv", dt2)], writes=[("pA", vlo // 512 + c)],
                    tag=f"v{i}_{dt2}_{c}")
        for dt2 in range(DT):
            lhs = xnT[:, dt2 * N + 128 * i: dt2 * N + 128 * i + 128]
            top(lambda e, l=lhs, d2=dt2, g2=glo: e.matmul(
                psB[:, g2:g2 + NE], l, wg_bf[:, d2 * NE:d2 * NE + NE],
                start=(d2 == 0), stop=(d2 == DT - 1)),
                reads=[("xnT", i), ("wg",)], writes=[("pB", glo // 512)], tag=f"g{i}_{dt2}")
        for dt2 in range(DT):
            lhs = xnT[:, dt2 * N + 128 * i: dt2 * N + 128 * i + 128]
            top(lambda e, l=lhs, d2=dt2, g2=glo: e.matmul(
                psB[:, g2 + NE:g2 + NE + ER], l, wu_bf[:, d2 * ER:d2 * ER + ER],
                start=(d2 == 0), stop=(d2 == DT - 1)),
                reads=[("xnT", i), ("wu",)], writes=[("pB", glo // 512)], tag=f"t{i}_{dt2}")

    def emit_E(i):
        r = i % 2
        vlo = r * 1024
        glo = 512 * (1 + r)
        vsrc = psA[:, vlo:vlo + 1024].rearrange("p (h v) -> p h v", v=64)
        vdst = vaug[:, i * VSTR:(i + 1) * VSTR].rearrange("p (h u) -> p h u", u=65)[:, :, 0:64]
        vop(lambda e, d=vdst, s=vsrc: e.tensor_copy(d, s),
            reads=[("pA", vlo // 512), ("pA", vlo // 512 + 1)],
            writes=[("vaug", i)], tag=f"vev{i}")
        vop(lambda e, i2=i, g2=glo: e.tensor_copy(tsb[:, i2 * ER:(i2 + 1) * ER],
                                                  psB[:, g2 + NE:g2 + NE + ER]),
            reads=[("pB", glo // 512)], writes=[("tsb", i)], tag=f"tev{i}")
        vop(lambda e, i2=i, g2=glo: e.tensor_copy(gate_raw[:, i2 * NE:(i2 + 1) * NE],
                                                  psB[:, g2:g2 + NE]),
            reads=[("pB", glo // 512)], writes=[("graw", i)], tag=f"gev{i}")

    for i in range(NT + 3):
        if i < NT:
            emit_A(i)
        if 1 <= i <= NT:
            emit_B(i - 1)
        if 3 <= i <= NT + 2:
            emit_C(i - 3)
            emit_E(i - 3)

    for t in range(NT):
        ap = vaug[:, t * VSTR:(t + 1) * VSTR].rearrange("p (h u) -> p h u", u=65)[:, :, 64:65]
        vop(lambda e, a=ap: e.memset(a, 1.0), writes=[("vone", t)], tag=f"vone{t}")

    sop(lambda e: e.activation(gate_all[:], gate_raw[:], AF.Exp, bias=zeroc[:, 0:1],
                               scale=1.0),
        reads=[("graw", i) for i in range(NT)] + [("zeroc",)], writes=[("gexp",)],
        tag="gexp")
    vop(lambda e: e.reduce_sum(gsum[:, 0:16],
                               gate_all[:].rearrange("p (t e) -> p t e", e=NE), AX.X),
        reads=[("gexp",)], writes=[("gsum",)], tag="gsum")
    vop(lambda e: e.reciprocal(gsum[:, 16:32], gsum[:, 0:16]),
        reads=[("gsum",)], writes=[("grec",)], tag="grec")
    vop(lambda e: e.tensor_tensor(
        gate_all[:].rearrange("p (t e) -> p t e", e=NE),
        gate_all[:].rearrange("p (t e) -> p t e", e=NE),
        gsum[:, 16:32].rearrange("p (t u) -> p t u", u=1).broadcast_to([128, NT, NE]),
        ALU.mult),
        reads=[("gexp",), ("grec",)], writes=[("gate", i) for i in range(NT)], tag="gfin")

    emit_wo_load()

    rot_counter = [0]
    emit_wqk_bar(0)
    emit_qkrow_init()
    for g in range(GROUPS):
        if g > 0:
            emit_wqk_dmas(g)
            emit_wqk_bar(g)
        for fi, ft in enumerate((0, 2, 1, 3)):
            isq = ft < 2
            ps, pk = (psA, [("pA", b) for b in range(4)]) if fi % 2 == 0 else \
                     (psB, [("pB", b) for b in range(4)])
            hA = 4 * g + (ft % 2) * 2
            hB = hA + 1
            slotA, slotB = hA - 4 * g, hB - 4 * g
            wcol = (0 if isq else 256) + (ft % 2) * 128
            for dt2 in range(DT):
                lhs = wqk_bf[:, dt2 * 512 + wcol: dt2 * 512 + wcol + 128]
                for c in range(4):
                    top(lambda e, p=ps, l=lhs, d2=dt2, c2=c: e.matmul(
                        p[:, 512 * c2:512 * c2 + 512], l,
                        xnT[:, d2 * N + 512 * c2: d2 * N + 512 * c2 + 512],
                        start=(d2 == 0), stop=(d2 == DT - 1)),
                        reads=[("wqkgrp", g), ("wqkbuf",)] + [("xnT", tt) for tt in range(NT)],
                        writes=pk, tag=f"qk{g}_{ft}_{dt2}_{c}")
            sop(lambda e, p=ps: e.copy(stage[:, 0:1024], p[:, 0:1024]),
                reads=pk, writes=[("stage",)], tag=f"stgA{g}_{ft}")
            vop(lambda e, p=ps: e.tensor_copy(stage[:, 1024:2048], p[:, 1024:2048]),
                reads=pk, writes=[("stage",)], tag=f"stgB{g}_{ft}")
            bufA = QT[slotA] if isq else KT[slotA]
            bufB = QT[slotB] if isq else KT[slotB]
            bkA = ("QTm" if isq else "KTm", slotA)
            bkB = ("QTm" if isq else "KTm", slotB)
            if head_is_rope(hA):
                dma("scalar", hbsB[0:64, :], stage[64:128, :], "d_hb",
                    reads=[("stage",)], writes=[("hbsB",)], tag=f"hb{g}_{ft}")
                dma("scalar", swapA[0:32, :], stage[32:64, :], "d_swA",
                    reads=[("stage",)], writes=[("swA",)], tag=f"sw1{g}_{ft}")
                dma("scalar", swapA[32:64, :], stage[0:32, :], "d_swA",
                    reads=[("stage",)], writes=[("swA",)], tag=f"sw2{g}_{ft}")
                dma("scalar", swapB[0:32, :], stage[96:128, :], "d_swB",
                    reads=[("stage",)], writes=[("swB",)], tag=f"sw3{g}_{ft}")
                dma("scalar", swapB[32:64, :], stage[64:96, :], "d_swB",
                    reads=[("stage",)], writes=[("swB",)], tag=f"sw4{g}_{ft}")
                for (base, swap, buf, bk, bask, swk) in (
                        (stage, swapA, bufA, bkA, ("stage",), ("swA",)),
                        (hbsB, swapB, bufB, bkB, ("hbsB",), ("swB",))):
                    vop(lambda e, b=buf, s=base: e.tensor_mul(
                        b[0:64, :], s[0:64, :], trigC[0:64, :]),
                        reads=[bask, ("trigC",)], writes=[bk], tag="rp_a")
                    vop(lambda e, s=swap, sk2=swk: e.tensor_mul(
                        ropt1[0:64, :], s[0:64, :], trigS[0:64, :]),
                        reads=[swk, ("trigS",)], writes=[("rt1",)], tag="rp_b")
                    vop(lambda e, b=buf: e.tensor_add(
                        b[0:64, :], ropt1[0:64, :], b[0:64, :]),
                        reads=[("rt1",), bk], writes=[bk], tag="rp_c")
            else:
                dma("scalar", bufB[0:64, :], stage[64:128, :], "d_hb",
                    reads=[("stage",)], writes=[bkB], tag=f"cpB{g}_{ft}")
                vop(lambda e, b=bufA: e.tensor_copy(b[0:64, :], stage[0:64, :]),
                    reads=[("stage",)], writes=[bkA], tag=f"cpA{g}_{ft}")
        for s_ in range(GH):
            h = 4 * g + s_
            if head_is_decay(h):
                vop(lambda e, s2=s_, h2=h: e.tensor_scalar(
                    QT[s2][64:65, :], posm[64:65, 0:N], invxib[64:65, h2:h2 + 1],
                    -8.0, ALU.mult, ALU.mult),
                    reads=[("posm",), ("invxib",), ("QTm", s_)],
                    writes=[("QTrow", s_)], tag=f"qrow{g}_{s_}")


        if _DEBUG and g == 0:
            dma("gpsimd", dbg["qt"], aQK[:], "d_misc",
                reads=[("QTm", s) for s in range(GH)] + [("KTm", s) for s in range(GH)]
                + [("QTrow", s) for s in range(GH)] + [("KTrow", s) for s in range(GH)],
                tag="dbgQ")

        for s_ in range(GH):
            for cch in range(4):
                h = 4 * g + s_
                decay = head_is_decay(h)
                qk_reads = [("QTm", s_), ("KTm", s_), ("QTrow", s_), ("KTrow", s_)]
                pend = []
                cs = 512 * cch
                for t in range(4 * cch + 4):
                    rot = rot_counter[0] % 3
                    rot_counter[0] += 1
                    plo = 512 * rot
                    sk = [("pB", rot)]
                    pre = max(0, 128 * t - cs)
                    top(lambda e, p=plo, t2=t, s2=s_, c2=cs: e.matmul(
                        psB[:, p:p + CW], KT[s2][0:65, 128 * t2:128 * t2 + 128],
                        QT[s2][0:65, c2:c2 + CW], start=True, stop=True),
                        reads=qk_reads, writes=sk, tag=f"sc{h}_{cch}_{t}")
                    if t >= 4 * cch:
                        vop(lambda e, p=plo, o=pre: e.tensor_add(
                            psB[:, p + o:p + o + 128], psB[:, p + o:p + o + 128],
                            masktri[:]),
                            reads=sk + [("masktri",)], writes=sk, tag=f"mk{h}_{cch}_{t}")
                    if decay:
                        bap = biasall[:, 16 * h + t:16 * h + t + 1]
                        br = [("biasall", h)]
                    else:
                        bap = biasc[:, 0:1]
                        br = [("biasc",)]
                    sop(lambda e, p=plo, b=bap, o=pre, r2=rot: e.activation(
                        numb[:, 512 * r2 + o:512 * r2 + CW], psB[:, p + o:p + CW],
                        AF.Exp, bias=b, scale=0.125),
                        reads=sk + br, writes=[("num", rot)], tag=f"ex{h}_{cch}_{t}")

                    def mk_attnv(t2, rot2, pre2, h2, cch2):
                        def emit():
                            for su in range(pre2 // 128, 4):
                                top(lambda e, su2=su, t3=t2, r2=rot2, h3=h2,
                                    st=(t2 == 4 * cch2 + su): e.matmul(
                                    psA[:, 512 * su2:512 * su2 + 65],
                                    numb[:, 512 * r2 + 128 * su2:512 * r2 + 128 * su2 + 128],
                                    vaug[:, t3 * VSTR + 65 * h3:t3 * VSTR + 65 * h3 + 65],
                                    start=(t3 == 0), stop=st),
                                    reads=[("num", rot2), ("vaug", t2), ("vone", t2)],
                                    writes=[("pA", su)], tag=f"av{h2}_{cch}_{t2}_{su}")
                        return emit
                    pend.append(mk_attnv(t, rot, pre, h, cch))
                    if len(pend) > 2:
                        pend.pop(0)()
                for fn in pend:
                    fn()
                rv = rden[:, (h % 2) * 4:(h % 2) * 4 + 4]
                den = psA[:].rearrange("p (s c) -> p s c", c=512)[:, :, 64:65]
                vop(lambda e, r2=rv, d=den: e.reciprocal(r2, d),
                    reads=[("pA", b) for b in range(4)], writes=[("rden", h % 2)],
                    tag=f"rcp{h}_{cch}")
                asrc = psA[:].rearrange("p (s c) -> p s c", c=512)[:, :, 0:64]
                adst = attok[:, cch * 4 * D:(cch + 1) * 4 * D].rearrange(
                    "p (s d) -> p s d", d=D)[:, :, 64 * h:64 * h + 64]
                rbc = rv.rearrange("p (s u) -> p s u", u=1).broadcast_to([128, 4, 64])
                vop(lambda e, d=adst, s2=asrc, r2=rbc: e.tensor_tensor(d, s2, r2, ALU.mult),
                    reads=[("pA", b) for b in range(4)] + [("rden", h % 2)],
                    writes=[("attok", cch, h)] + [("xnk", 4 * cch + j) for j in range(4)],
                    tag=f"aev{h}_{cch}")
                if _DEBUG and g == 0 and cch == 0 and s_ == 0:
                    dma("gpsimd", dbg["rden"], rden[:], "d_misc",
                        reads=[("rden", h % 2)], tag="dbgR")

    emit_wff_load()

    allqk = [("QTm", s) for s in range(GH)] + [("KTm", s) for s in range(GH)] + \
            [("QTrow", s) for s in range(GH)] + [("KTrow", s) for s in range(GH)]
    for s3 in range(NT):
        src = attok[:, s3 * D:(s3 + 1) * D]
        if s3 < 8:
            dstT = aS[:, 1024 * s3:1024 * s3 + 1024].rearrange(
                "p (b r) -> p b r", r=128)
            wk = [("attT", s3), ("xbar",),
                  ("stage",), ("hbsB",), ("swA",), ("swB",)]
        else:
            dstT = aQK[:, 1024 * (s3 - 8):1024 * (s3 - 8) + 1024].rearrange(
                "p (b r) -> p b r", r=128)
            wk = [("attT", s3), ("xbar",)] + (allqk if s3 == 8 else [])
        dma("sync", dstT, src, "d_sync",
            reads=[("attok", s3 // 4, h) for h in range(HEADS)] + [("xbar",)],
            writes=wk, tag=f"aT{s3}", transpose=True)

    if _DEBUG:
        dma("gpsimd", dbg["attok"], attok[:], "d_misc",
            reads=[("attok", c, h) for c in range(4) for h in range(HEADS)], tag="dbgA")
        dbg["attTa"] = nc.dram_tensor("dbg_attTa", [128, 8192], BF16, kind="ExternalOutput").ap()
        dbg["attTb"] = nc.dram_tensor("dbg_attTb", [128, 8192], BF16, kind="ExternalOutput").ap()
        dma("gpsimd", dbg["attTa"], aS[:, 0:8192], "d_misc",
            reads=[("attT", i) for i in range(8)], tag="dbgTa")
        dma("gpsimd", dbg["attTb"], aQK[:, 0:8192], "d_misc",
            reads=[("attT", i) for i in range(8, 16)], tag="dbgTb")
        dma("gpsimd", dbg["xnT"], xnT[:], "d_misc",
            reads=[("xnT", i) for i in range(NT)], tag="dbgX")
        dma("gpsimd", dbg["gate"], gate_all[:], "d_misc",
            reads=[("gate", i) for i in range(NT)], tag="dbgG")
        dma("gpsimd", dbg["tsb"], tsb[:], "d_misc",
            reads=[("tsb", i) for i in range(NT)], tag="dbgT")
        dma("gpsimd", dbg["vaug"], vaug[:], "d_misc",
            reads=[("vaug", i) for i in range(NT)] + [("vone", i) for i in range(NT)],
            tag="dbgV")

    def emit_tg(j):
        rj = j % 2
        tgvj = tg_b[:, rj * ER:rj * ER + ER]
        tgTj = tgT_b[:, rj * ER:rj * ER + ER]
        sj = rj * 1024
        for e2 in range(NE):
            vop(lambda e, v=tgvj, i2=j, e3=e2: e.tensor_scalar(
                v[:, RK * e3:RK * e3 + RK],
                tsb[:, i2 * ER + RK * e3:i2 * ER + RK * e3 + RK],
                gate_all[:, i2 * NE + e3:i2 * NE + e3 + 1], 1.0 / RK, ALU.mult, ALU.mult),
                reads=[("tsb", j), ("gate", j)],
                writes=[("tg", rj)], tag=f"tg{j}_{e2}")
        top(lambda e, v=tgvj, s2=sj: e.matmul(
            psA[0:64, s2:s2 + ER], v[:, 0:64], identb[:], start=True, stop=True),
            reads=[("tg", rj), ("identb",)], writes=[("pA", sj // 512)],
            tag=f"tgM{j}a")
        top(lambda e, v=tgvj, s2=sj: e.matmul(
            psA[64:128, s2:s2 + ER], v[:, 64:128], identb[:], start=True, stop=True),
            reads=[("tg", rj), ("identb",)], writes=[("pA", sj // 512)],
            tag=f"tgM{j}b")
        sop(lambda e, o=tgTj, s2=sj: e.activation(
            o, psA[:, s2:s2 + ER], AF.Copy, bias=0.0, scale=1.0),
            reads=[("pA", sj // 512)], writes=[("tgT", rj)],
            tag=f"tgE{j}")

    nstore = 0
    for i in range(NT):
        r = i % 2
        xrv = xr_f32[:, r * D:r * D + D]
        xr_extra = [("vaug", t) for t in range(4)] + [("vone", t) for t in range(4)] \
            if i < 2 else []
        dma("scalar", xrv, x_t[i], "d_x0" if i % 2 == 0 else "d_x1",
            reads=[("xt2", r)], writes=[("xrl", i)] + xr_extra, tag=f"xrl{i}")
        slo = r * 1024
        tgTv = tgT_b[:, r * ER:r * ER + ER]
        if i == 0:
            emit_tg(0)
        attTb = attT0 if i < 8 else attT1
        abase = 1024 * i if i < 8 else 1024 * (i - 8)
        for dt2 in range(DT):
            lhs = attTb[:, abase + 128 * dt2: abase + 128 * dt2 + 128]
            for c in range(2):
                top(lambda e, l=lhs, d2=dt2, c2=c, s2=slo: e.matmul(
                    psA[:, s2 + 512 * c2:s2 + 512 * c2 + 512], l,
                    wo_bf[:, d2 * D + 512 * c2:d2 * D + 512 * c2 + 512],
                    start=(d2 == 0), stop=(d2 == DT - 1)),
                    reads=[("attT", i), ("wo",)], writes=[("pA", slo // 512 + c)],
                    tag=f"wo{i}_{dt2}_{c}")
        if i + 1 < NT:
            emit_tg(i + 1)
        blo = r * 1024
        for dt2 in range(DT):
            lhs = xnT[:, dt2 * N + 128 * i: dt2 * N + 128 * i + 128]
            for c in range(2):
                top(lambda e, l=lhs, d2=dt2, c2=c, b=blo: e.matmul(
                    psB[:, b + 512 * c2:b + 512 * c2 + 512], l,
                    wff_bf[:, d2 * D + 512 * c2:d2 * D + 512 * c2 + 512],
                    start=(d2 == 0), stop=False),
                    reads=[("xnT", i), ("wff",)], writes=[("pB", blo // 512 + c)],
                    tag=f"ba{i}_{dt2}_{c}")
        for c in range(2):
            top(lambda e, v=tgTv, c2=c, b=blo: e.matmul(
                psB[:, b + 512 * c2:b + 512 * c2 + 512], v,
                wvl_bf[:, 512 * c2:512 * c2 + 512], start=False, stop=True),
                reads=[("tgT", r), ("wvl",)], writes=[("pB", blo // 512 + c)],
                tag=f"lo{i}_{c}")
        gv = gbuf0[:, 0:D] if r == 0 else gbuf1[:, 0:D]
        gk = ("wqkbuf",)
        sop(lambda e, gg=gv, b=blo: e.activation(gg, psB[:, b:b + D], AF.Silu,
                                                 bias=zeroc[:, 0:1], scale=1.0),
            reads=[("pB", blo // 512), ("pB", blo // 512 + 1), ("zeroc",)],
            writes=[("g", r), gk], tag=f"si{i}")
        vop(lambda e, gg=gv, s2=slo: e.tensor_tensor(gg, psA[:, s2:s2 + D], gg, ALU.mult),
            reads=[("pA", slo // 512), ("pA", slo // 512 + 1), ("g", r)],
            writes=[("g", r)], tag=f"sg{i}")
        vop(lambda e, o=xrv, gg=gv: e.tensor_tensor(o, gg, o, ALU.add),
            reads=[("g", r), ("xrl", i)], writes=[("outv", r)], tag=f"res{i}")
        dma("scalar", y_t[i], xrv, "d_scal", reads=[("outv", r)],
            writes=[("xt2", r)], tag=f"st{i}")
        nstore += 1

    dma_count = P.finalize()
    fin = [("sync", "d_scal", dma_count["d_scal"])]

    per_engine = {e: [rec for rec in P.ops if rec["engine"] == e] for e in ENGINES}

    with nc.Block() as block:
        def run_engine(recs, fins):
            def body(eng):
                for rec in recs:
                    for sn, v in rec["waits"]:
                        eng.wait_ge(sems[sn], v)
                    inst = rec["builder"](eng)
                    if rec["inc"] is not None:
                        sn, amt = rec["inc"]
                        inst.then_inc(sems[sn], amt)
                for sn, v in fins:
                    eng.wait_ge(sems[sn], v)
            return body
        block.tensor(run_engine(per_engine["tensor"], []))
        block.scalar(run_engine(per_engine["scalar"], []))
        block.vector(run_engine(per_engine["vector"], []))
        block.gpsimd(run_engine(per_engine["gpsimd"], []))
        block.sync(run_engine(per_engine["sync"],
                              [(sn, v) for (_, sn, v) in fin]))

    for cm in reversed(ctx):
        cm.__exit__(None, None, None)

    nops = {e: len(v) for e, v in per_engine.items()}
    return nc, nops


def kernel(L0_attn_x, L0_attn_w_qkv, L0_attn_w_o, L0_attn_log_xi,
           L0_ffn_weight, L0_ffn_gate_w, L0_ffn_lora_U, L0_ffn_lora_V):
    x = np.ascontiguousarray(np.asarray(L0_attn_x, dtype=np.float32))
    B = x.shape[0]
    wd = prep_weights(L0_attn_w_qkv, L0_attn_w_o, L0_attn_log_xi, L0_ffn_weight,
                      L0_ffn_gate_w, L0_ffn_lora_U, L0_ffn_lora_V)
    cd = build_consts()
    nc, _ = build()
    in_maps = []
    for b in range(B):
        m = {"x": np.ascontiguousarray(x[b])}
        m.update(wd)
        m["w_qk"] = wd["w_qk"]
        m["logxi"] = wd["logxi"]
        m.update(cd)
        in_maps.append(m)
    res = run_bass_kernel_spmd(nc, in_maps, core_ids=list(range(B)))
    out = np.stack([np.asarray(res.results[b]["y"]) for b in range(B)], axis=0)
    return out.astype(np.float32)
